# version 24
# baseline (speedup 1.0000x reference)
import sys

for _p in ("/opt/trn_rl_repo",):
    if _p not in sys.path:
        sys.path.insert(0, _p)

import contextlib
import numpy as np
import ml_dtypes

import concourse.bass as bass
import concourse.mybir as mybir
from concourse.bass_utils import run_bass_kernel_spmd
from concourse.alu_op_type import AluOpType as OP

F32 = mybir.dt.float32
BF16 = mybir.dt.bfloat16
U32 = mybir.dt.uint32
AF = mybir.ActivationFunctionType
AX = mybir.AxisListType.X

B, T, D, H = 4, 8192, 1024, 8
HG = 4
KD = 128
F = 4096
TILE = 512
CH = 128
NCH = TILE // CH
DT = D // 128
FT = F // 128
EPS_LN = 1e-5
EPS_L2 = 1e-6
bfd = ml_dtypes.bfloat16


class Sem:
    def __init__(self, nc, es, name):
        self.h = es.enter_context(nc.semaphore(name))
        self.n = 0

    def inc(self, instr, k=1):
        instr.then_inc(self.h, k)
        self.n += k

    def wait(self, eng, n=None):
        eng.wait_ge(self.h, self.n if n is None else n)


class Builder:
    def __init__(self, nta=T // TILE, do_b=True, taps=(), tapsB=(), dbg_dram=False):
        self.nta = nta
        self.ntb = (nta * TILE // 2) // TILE if do_b else 0
        self.do_b = do_b
        self.taps = list(taps)
        self.tapsB = list(tapsB)
        self.dbg_dram = dbg_dram

    def build(self):
        nc = bass.Bass("TRN2", target_bir_lowering=False)
        self.nc = nc
        self._dram()
        es = contextlib.ExitStack()
        self.es = es
        with es:
            self._alloc_persistent()
            self.esA = contextlib.ExitStack()
            with self.esA:
                self._alloc_A()
                self.block = es.enter_context(nc.Block())
                self._phase0()
                self._sign_step(0)
                self.sdv_solve = {}
                self.sdv_snew = {}
                self.fdv = {}
                for f in self._front_pieces(0):
                    f()
                for f in self._chunk_solve(0, 0):
                    f()
                for t in range(self.nta):
                    pieces = self._front_pieces(t + 1) if t + 1 < self.nta else [None] * NCH
                    for g_ in range(NCH):
                        if pieces[g_] is not None:
                            pieces[g_]()
                        c = t * NCH + g_
                        if c + 1 < self.nta * NCH:
                            nt_, ng = divmod(c + 1, NCH)
                            SA = self._chunk_solve(nt_, ng)
                        else:
                            SA = []
                        CB = self._chunk_chain(t, g_)
                        order = list(SA[0:4])
                        k = 4
                        for cb in CB:
                            order.append(cb)
                            if k < len(SA):
                                order.append(SA[k])
                                k += 1
                        order += SA[k:]
                        for f in order:
                            f()
                    self.bf_(f"ifpe{t % 2}", self.s_pe)
                    self.bf_(f"ifdv{t % 2}", self.s_dv)
                    self._sign_step(t + 1)
                    if t < 3:
                        self._casts(t)
                    if self.do_b and t == self.nta // 2 - 1:
                        self._exchange_half()
                if self.do_b:
                    for t in range(self.nta + 1, DT + 1):
                        self._sign_step(t)
                    self._exchange()
                self._tapA()
            if self.do_b:
                self._alloc_B()
                self.tlB = {}
                self.wbv = {}
                self._loadB(0)
                for t in range(self.ntb):
                    self._tileB(t)
            self._finish()
        return nc

    def _dram(self):
        nc = self.nc
        dr = lambda n, s, d, k="ExternalInput": nc.dram_tensor(n, s, d, kind=k).ap()
        nta, ntb = self.nta, self.ntb
        self.d_x = dr("x_full", [nta * TILE, D], F32)
        self.d_wq = dr("wq", [D, HG * KD], F32)
        self.d_wk = dr("wk", [D, HG * KD], F32)
        self.d_wv = dr("wv", [D, HG * KD], F32)
        self.d_wab = dr("wab", [D, 2 * HG], F32)
        self.d_ln1w = dr("ln1w", [1, D], F32)
        self.d_ln1b = dr("ln1b", [1, D], F32)
        self.d_ident = dr("ident_bf", [128, 128], BF16)
        self.d_identf = dr("ident_f32", [128, 128], F32)
        self.d_onescol = dr("ones_col", [128, 1], BF16)
        self.d_onesrowb = dr("ones_row_bf", [1, 128], BF16)
        self.d_onesrowf = dr("ones_row_f32", [1, 128], F32)
        self.d_ones4 = dr("ones4_f32", [4, 128], F32)
        self.d_hmask = dr("hmask_f32", [4, HG * 128], F32)
        self.d_mask_ls = dr("mask_ls", [128, 128], BF16)
        self.d_mask_us = dr("mask_us", [128, 128], BF16)
        self.d_mask_ui = dr("mask_ui", [128, 128], BF16)
        idr = lambda n, s, d: nc.dram_tensor(n, s, d).ap()
        self.d_xn = idr("xn_spill", [nta * TILE, D], BF16)
        self.d_n = idr("n_spill", [nta * TILE, D], BF16)
        self.d_kfm = idr("kfm_spill", [nta, HG, 128, TILE], BF16)
        self.d_rec = idr("rec_spill", [nta * TILE, HG * KD], BF16)
        if self.do_b:
            self.d_xown = dr("x_own", [ntb * TILE, D], F32)
            self.d_wo = dr("wo", [D, D], F32)
            self.d_w1 = dr("w1", [D, F], F32)
            self.d_w2 = dr("w2", [F, D], F32)
            self.d_b1c = dr("b1c", [128, FT], F32)
            self.d_b2c = dr("b2c", [128, DT], F32)
            self.d_ln2wc = dr("ln2wc", [128, DT], F32)
            self.d_ln2bc = dr("ln2bc", [128, DT], F32)
            self.d_ffnw = dr("ffnw", [D, D], F32)
            self.d_poff = dr("poff", [1, 1], U32)
            self.d_poff2 = dr("poff2", [1, 1], U32)
            self.d_out = dr("out", [ntb * TILE, D], F32, k="ExternalOutput")
            self.d_ag = idr("ag_out", [2 * nta * TILE, HG * KD], BF16)
            self.d_oown = idr("o_own", [ntb * TILE, D], BF16)
            self.d_xnown = idr("xn_own", [ntb * TILE, D], BF16)
            self.d_w1b = idr("w1_bf", [8, 128, DT, 512], BF16)
            self.d_w2b = idr("w2_bf", [DT, 128, FT, 128], BF16)
            self.d_swb = idr("sw_bf", [D, D], BF16)
            self.d_sg = idr("sg_spill", [ntb, D, TILE], BF16)
        self.d_taps = {}
        for (name, _src, shape, dt) in list(self.taps) + list(self.tapsB):
            self.d_taps[name] = dr("tap_" + name, shape, dt, k="ExternalOutput")

    def _alloc_persistent(self):
        nc, es = self.nc, self.es
        sb = lambda n, s, d: es.enter_context(nc.sbuf_tensor(n, s, d))
        ps = lambda n, s: es.enter_context(nc.psum_tensor(n, s, F32))
        self.ident = sb("ident", [128, 128], BF16)
        self.identf = sb("identf", [128, 128], F32)
        self.onescol = sb("onescol", [128, 1], BF16)
        self.onesrowb = sb("onesrowb", [1, 128], BF16)
        self.onesrowf = sb("onesrowf", [1, 128], F32)
        self.ones4 = sb("ones4", [4, 128], F32)
        self.hmask = sb("hmask", [4, HG * 128], F32)
        self.mls = sb("mls", [128, 128], BF16)
        self.mus = sb("mus", [128, 128], BF16)
        self.mui = sb("mui", [128, 128], BF16)
        self.epscol = sb("epscol", [128, 2], F32)
        if self.do_b:
            self.wo = sb("wo_sb", [128, DT, D], BF16)
            self.swt = sb("swt_sb", [128, DT, D], BF16)
            self.b1c = sb("b1c_sb", [128, FT], F32)
            self.b2c = sb("b2c_sb", [128, DT], F32)
            self.ln2wc = sb("ln2wc_sb", [128, DT], F32)
            self.ln2bc = sb("ln2bc_sb", [128, DT], F32)
        self.pb0 = ps("pb0", [128, 1024])
        self.pb1 = ps("pb1", [128, 1024])
        self.p0 = ps("p0", [128, 512])
        self.p1 = ps("p1", [128, 512])
        self.p2 = ps("p2", [128, 512])
        self.p3 = ps("p3", [128, 512])
        S_ = lambda n: Sem(nc, es, n)
        self.s_wc = S_("s_wc")
        self.s_sn = S_("s_sn")
        self.s_ta = [S_("s_ta0"), S_("s_ta1")]
        self.s_w1 = [S_("s_w1a"), S_("s_w1b")]
        self.s_w2 = [S_("s_w2a"), S_("s_w2b")]
        self.s_ld = S_("s_ld")
        self.s_x = S_("s_x")
        self.s_sp = S_("s_sp")
        self.s_tl = S_("s_tl")
        self.s_pe = S_("s_pe")
        self.s_dv = S_("s_dv")
        self.s_ac = S_("s_ac")
        self.s_pl = S_("s_pl")
        self.s_cc = S_("s_cc")
        self.s_st = S_("s_st")
        self.s_wb = S_("s_wb")
        self.guard = {}

    def _alloc_A(self):
        nc, es = self.nc, self.esA
        sb = lambda n, s, d: es.enter_context(nc.sbuf_tensor(n, s, d))
        self.wq = sb("wq_sb", [128, DT, HG * KD], BF16)
        self.wk = sb("wk_sb", [128, DT, HG * KD], BF16)
        self.wv = sb("wv_sb", [128, DT, HG * KD], BF16)
        self.wab = sb("wab_sb", [128, DT, 2 * HG], BF16)
        self.ln1w_bc = sb("ln1w_bc", [128, D], BF16)
        self.ln1b_bc = sb("ln1b_bc", [128, D], BF16)
        self.ln1row = sb("ln1row", [1, 2 * D], BF16)
        self.xs = sb("xs", [128, 2, D], F32)
        self.stat = sb("stat", [128, 32], F32)
        self.bnb = sb("bnb", [128, 2, 12], F32)
        self.xnb = sb("xnb", [128, NCH, D], BF16)
        self.nn = sb("nn", [128, NCH, D], BF16)
        self.nT = sb("nT", [128, DT, TILE], BF16)
        self.qh = sb("qh", [128, 2, HG, TILE], BF16)
        self.kh = sb("kh", [128, 2, HG, TILE], BF16)
        self.vt = sb("vt", [128, 2, NCH, HG * KD], BF16)
        self.ktm = sb("ktm", [128, 2, HG, NCH, 128], BF16)
        self.sqs = sb("sqs", [128, 8 * TILE], BF16)
        self.rowA = sb("rowA", [1, TILE], F32)
        self.rowB = sb("rowB", [1, TILE], BF16)
        self.pa = sb("pa", [4, 2, TILE], F32)
        self.pbr = sb("pbr", [4, 2, TILE], F32)
        self.pcr = sb("pcr", [4, 2, TILE], F32)
        self.sfx = sb("sfx", [4, 2, TILE], F32)
        self.cmb = sb("cmb", [4, 2, TILE], F32)
        self.rmp = sb("rmp", [4, HG * 128], F32)
        self.rmn = sb("rmn", [4, HG * 128], F32)
        self.rmc = sb("rmc", [4, HG * 128], F32)
        self.ccol = sb("ccol", [128, 2, NCH, 16], F32)
        self.dcol = sb("dcol", [128, 2, NCH, 28], F32)
        self.argf = sb("argf", [128, HG * 128], F32)
        self.eM = sb("eM", [128, HG * 128], BF16)
        self.tp = sb("tp", [128, HG * 128], BF16)
        self.Mx = sb("Mx", [128, HG * 128], BF16)
        self.MTx = sb("MTx", [128, HG * 128], BF16)
        self.NTx = sb("NTx", [128, 2, HG * 128], BF16)
        self.Pw = [sb(f"Pw{i}", [128, HG * 128], BF16) for i in range(7)]
        self.Ra = sb("Ra", [128, HG, 256], BF16)
        self.Rb = sb("Rb", [128, HG, 256], BF16)
        self.WmT = sb("WmT", [128, 2, HG, 128], BF16)
        self.Ufree = sb("Ufree", [128, 2, HG, 128], BF16)
        self.Ug = sb("Ug", [128, HG, 128], BF16)
        self.Uu = sb("Uu", [128, HG, 128], BF16)
        self.Sst = [sb("Sst0", [128, HG, 128], BF16), sb("Sst1", [128, HG, 128], BF16)]
        self.t_o = sb("t_o", [128, HG * 128], F32)
        self.Ox = sb("Ox", [128, HG * 128], BF16)
        self.wldb = sb("wldb", [128, 1024], BF16)
        self.wld = sb("wld", [128, 1024], F32)

    def _alloc_B(self):
        nc, es = self.nc, self.es
        sb = lambda n, s, d: es.enter_context(nc.sbuf_tensor(n, s, d))
        self.oT = sb("oT", [128, 2, DT, TILE], BF16)
        self.xnT = sb("xnT", [128, 2, DT, TILE], BF16)
        self.ybf = sb("ybf", [128, DT, TILE], BF16)
        self.hT = sb("hT", [128, DT, TILE], BF16)
        self.hgx = sb("hgx", [128, FT, TILE], BF16)
        self.gsb = sb("gsb", [128, DT, TILE], BF16)
        self.w1s = sb("w1s", [128, 2, DT, 512], BF16)
        self.w2s = sb("w2s", [128, 2, FT, 128], BF16)
        self.rw = sb("rw", [1, 4, TILE], F32)
        self.bca = sb("bca", [128, TILE], F32)
        self.bcc = sb("bcc", [128, TILE], F32)
        self.tt1 = sb("tt1", [128, TILE], BF16)
        self.tt2 = sb("tt2", [128, TILE], BF16)
        self.tt3 = sb("tt3", [128, TILE], BF16)
        self.xb2 = sb("xb2", [128, 2, D], F32)
        self.outb = sb("outb", [128, 2, D], F32)
        self.sgT = sb("sgT", [128, NCH, D], BF16)

    def load_act(self, sc, name):
        from concourse.hw_specs import get_activation_tables
        if not hasattr(self, "_act_names"):
            self._act_names = list(get_activation_tables(self.nc.m.arch).keys())
        idx = self._act_names.index(name)
        return sc.add_instruction(mybir.InstLoadActFuncSet(
            name=self.nc.get_next_instruction_name(), act_func_set_id=idx, ins=[], outs=[]))

    def vchain(self, eng, sem, ops):
        for op in ops:
            sem.inc(op())
            eng.wait_ge(sem.h, sem.n)

    def bw(self, eng, name):
        g = self.guard.get(name)
        if g is not None:
            eng.wait_ge(g[0].h, g[1])

    def bf_(self, name, sem):
        self.guard[name] = (sem, sem.n)

    def _phase0(self):
        s_ld, s_sp, s_tl, s_ac, s_dv, s_pe = (self.s_ld, self.s_sp, self.s_tl,
                                              self.s_ac, self.s_dv, self.s_pe)

        @self.block.gpsimd
        def _(g):
            s_ld.inc(g.dma_start(out=self.wq[:, :, :], in_=self.d_wq.rearrange("(m p) c -> p m c", p=128)), 16)
            s_ld.inc(g.dma_start(out=self.wk[:, :, :], in_=self.d_wk.rearrange("(m p) c -> p m c", p=128)), 16)
            s_ld.inc(g.dma_start(out=self.wv[:, :, :], in_=self.d_wv.rearrange("(m p) c -> p m c", p=128)), 16)
            s_ld.inc(g.dma_start(out=self.wab[:, :, :], in_=self.d_wab.rearrange("(m p) c -> p m c", p=128)), 16)
            s_ld.inc(g.dma_start(out=self.ln1row[0:1, 0:D], in_=self.d_ln1w[0:1, :]), 16)
            s_ld.inc(g.dma_start(out=self.ln1row[0:1, D:2 * D], in_=self.d_ln1b[0:1, :]), 16)
            if self.do_b:
                s_ld.inc(g.dma_start(out=self.wo[:, :, :], in_=self.d_wo.rearrange("(m p) c -> p m c", p=128)), 16)
        self.ld_essential = s_ld.n

        @self.block.sync
        def _(sy):
            for src, dst in [
                (self.d_ident, self.ident), (self.d_identf, self.identf),
                (self.d_onescol, self.onescol), (self.d_onesrowb, self.onesrowb),
                (self.d_onesrowf, self.onesrowf), (self.d_ones4, self.ones4),
                (self.d_hmask, self.hmask), (self.d_mask_ls, self.mls),
                (self.d_mask_us, self.mus), (self.d_mask_ui, self.mui),
            ]:
                s_ld.inc(sy.dma_start(out=dst[:, :], in_=src[:, :]), 16)
            if self.do_b:
                for src, dst in [(self.d_b1c, self.b1c), (self.d_b2c, self.b2c),
                                 (self.d_ln2wc, self.ln2wc), (self.d_ln2bc, self.ln2bc)]:
                    s_ld.inc(sy.dma_start(out=dst[:, :], in_=src[:, :]), 16)
        self.ld_consts = s_ld.n

        @self.block.tensor
        def _(te):
            s_ld.wait(te, self.ld_consts)
            te.matmul(self.pb0[:, 0:512], self.onesrowb[0:1, :], self.ln1row[0:1, 0:512], start=True, stop=True)
            te.matmul(self.pb0[:, 512:1024], self.onesrowb[0:1, :], self.ln1row[0:1, 512:1024], start=True, stop=True)
            te.matmul(self.pb1[:, 0:512], self.onesrowb[0:1, :], self.ln1row[0:1, D:D + 512], start=True, stop=True)
            s_pe.inc(te.matmul(self.pb1[:, 512:1024], self.onesrowb[0:1, :], self.ln1row[0:1, D + 512:2 * D],
                               start=True, stop=True))

        @self.block.vector
        def _(v):
            v.memset(self.epscol[:, 0:1], EPS_LN)
            v.memset(self.epscol[:, 1:2], EPS_L2)
            v.memset(self.Sst[0][:, :, :], 0.0)
            v.memset(self.Sst[1][:, :, :], 0.0)
            s_pe.wait(v)
            v.tensor_copy(self.ln1w_bc[:, :], self.pb0[:, :])
            s_dv.inc(v.tensor_copy(self.ln1b_bc[:, :], self.pb1[:, :]))
        self.bf_("pb0", s_dv)
        self.bf_("pb1", s_dv)

        @self.block.scalar
        def _(sc):
            self.load_act(sc, "natural_log_exp_and_others")

    def _casts(self, part):
        if not self.do_b:
            return

        @self.block.gpsimd
        def _(g, part=part):
            if part == 0:
                for fb in range(8):
                    self.s_wc.inc(g.dma_start(
                        out=self.d_w1b[fb],
                        in_=self.d_w1[:, fb * 512:(fb + 1) * 512]
                        .rearrange("(m p) c -> p m c", p=128)), 16)
            else:
                for jd in range((part - 1) * 4, part * 4):
                    self.s_wc.inc(g.dma_start(
                        out=self.d_w2b[jd],
                        in_=self.d_w2[:, jd * 128:(jd + 1) * 128]
                        .rearrange("(i p) c -> p i c", p=128)), 16)

    def _sign_step(self, t):
        if not self.do_b or t > DT:
            return
        s_x, s_ac, s_sp = self.s_x, self.s_ac, self.s_sp

        @self.block.scalar
        def _(sc, t=t):
            if t >= 1:
                i = t - 1
                self.s_sn.wait(sc)
                self.bw(sc, "wldb")
                s_ac.inc(sc.activation(self.wldb[:, :], self.wld[:, :], AF.Sign))
                self.bf_("wld", s_ac)
                s_ac.wait(sc)
                s_sp.inc(sc.dma_start(out=self.d_swb[i * 128:(i + 1) * 128, :], in_=self.wldb[:, :]), 16)
                self.bf_("wldb", s_sp)
                if i == DT - 1:
                    self.bf_("swb_done", s_sp)
            if t < DT:
                self.bw(sc, "wld")
                self.s_sn.inc(sc.dma_start(out=self.wld[:, :], in_=self.d_ffnw[t * 128:(t + 1) * 128, :]), 16)

    def _front_pieces(self, t):
        def p0(t=t):
            self._ln_sub(t, 0)
            self._ln_sub(t, 1)

        def p1(t=t):
            self._ln_sub(t, 2)
            self._ln_sub(t, 3)
            self.bf_("xnb", self.s_sp)
            self.bf_("nn", self.s_sp)
            self._nt_load(t)

        def p2(t=t):
            self._proj(t)

        def p3(t=t):
            self._l2ab(t)
        return [p0, p1, p2, p3]

    def _ln_sub(self, t, s):
        s_x, s_sp, s_dv, s_ac = self.s_x, self.s_sp, self.s_dv, self.s_ac
        r0 = t * TILE
        par = s % 2

        @self.block.sync
        def _(sy, s=s, par=par):
            self.bw(sy, f"xs{par}")
            s_x.inc(sy.dma_start(out=self.xs[:, par, :],
                                 in_=self.d_x[r0 + s * 128: r0 + (s + 1) * 128, :]), 16)

            @self.block.vector
            def _(v, s=s, par=par):
                s_x.wait(v)
                o = par * 16
                st = lambda a, b: self.stat[:, o + a:o + b]
                x = self.xs[:, par, :]
                bn = self.bnb[:, par, :]
                v.bn_stats(bn[:, 0:6], x[:, 0:512])
                s_dv.inc(v.bn_stats(bn[:, 6:12], x[:, 512:1024]))
                v.wait_ge(s_dv.h, s_dv.n)
                s_dv.inc(v.bn_aggr(st(2, 4), bn.rearrange("p (g c) -> p g c", g=2)))

            @self.block.scalar
            def _(sc, s=s, par=par):
                s_dv.wait(sc)
                o = par * 16
                st = lambda a, b: self.stat[:, o + a:o + b]
                self.vchain(sc, s_ac, [
                    lambda: sc.activation(st(6, 7), st(3, 4), AF.Ln, bias=self.epscol[:, 0:1]),
                    lambda: sc.activation(st(7, 8), st(6, 7), AF.Exp, scale=-0.5),
                ])

            @self.block.vector
            def _(v, s=s, par=par):
                s_ac.wait(v)
                self.bw(v, "xnb")
                o = par * 16
                st = lambda a, b: self.stat[:, o + a:o + b]
                x = self.xs[:, par, :]
                xnb = self.xnb[:, s, :]
                bn = self.bnb[:, par, :]
                self.vchain(v, s_dv, [
                    lambda: v.tensor_scalar(xnb, x, st(2, 3), st(7, 8), OP.subtract, OP.mult),
                    lambda: v.bn_stats(bn[:, 0:6], xnb[:, 0:512]),
                ])
                s_dv.inc(v.bn_stats(bn[:, 6:12], xnb[:, 512:1024]))
                v.wait_ge(s_dv.h, s_dv.n)
                s_dv.inc(v.bn_aggr(st(10, 12), bn.rearrange("p (g c) -> p g c", g=2)))

            @self.block.scalar
            def _(sc, s=s, par=par):
                s_dv.wait(sc)
                o = par * 16
                st = lambda a, b: self.stat[:, o + a:o + b]
                self.vchain(sc, s_ac, [
                    lambda: sc.activation(st(14, 15), st(11, 12), AF.Ln, bias=self.epscol[:, 0:1]),
                    lambda: sc.activation(st(15, 16), st(14, 15), AF.Exp, scale=-0.5),
                ])

            @self.block.vector
            def _(v, s=s, par=par):
                s_ac.wait(v)
                self.bw(v, "nn")
                o = par * 16
                st = lambda a, b: self.stat[:, o + a:o + b]
                scr = self.xs[:, par, :]
                v.tensor_scalar(scr, self.xnb[:, s, :], st(10, 11),
                                st(15, 16), OP.subtract, OP.mult)
                v.tensor_tensor(self.nn[:, s, :], scr, self.ln1w_bc[:, :], OP.mult)
                s_dv.inc(v.tensor_tensor(self.nn[:, s, :], self.nn[:, s, :], self.ln1b_bc[:, :], OP.add))
                self.bf_(f"xs{par}", s_dv)

        @self.block.gpsimd
        def _(g, s=s):
            s_dv.wait(g)
            s_sp.inc(g.dma_start(out=self.d_xn[r0 + s * 128:r0 + (s + 1) * 128, :], in_=self.xnb[:, s, :]), 16)
            s_sp.inc(g.dma_start(out=self.d_n[r0 + s * 128:r0 + (s + 1) * 128, :], in_=self.nn[:, s, :]), 16)

    def _nt_load(self, t):
        s_sp, s_tl = self.s_sp, self.s_tl
        r0 = t * TILE

        @self.block.sync
        def _(sy):
            s_sp.wait(sy)
            self.bw(sy, "nT")
            s_tl.inc(sy.dma_start(out=self.nT[:, :, :], in_=self.d_n[r0:r0 + TILE, :], transpose=True), 16)

    def _proj(self, t):
        s_tl, s_pe, s_dv, s_ac = self.s_tl, self.s_pe, self.s_dv, self.s_ac
        tp = t % 2
        ifp, ifd = f"ifpe{tp}", f"ifdv{tp}"

        projdefs = [(self.qh, self.wq, 0), (self.qh, self.wq, 1),
                    (self.kh, self.wk, 0), (self.kh, self.wk, 1)]
        for idx, (dst, w, rnd) in enumerate(projdefs):
            pbank = self.pb0 if idx % 2 == 0 else self.pb1
            pname = "pb0" if idx % 2 == 0 else "pb1"

            @self.block.tensor
            def _(te, dst=dst, w=w, rnd=rnd, pbank=pbank, pname=pname):
                s_tl.wait(te)
                self.s_ld.wait(te, self.ld_essential)
                self.bw(te, pname)
                if pname == "pb0":
                    self.bw(te, "pb0h")
                    self.bw(te, "pb0c")
                ins = None
                for hh in range(2):
                    h = rnd * 2 + hh
                    for m in range(DT):
                        ins = te.matmul(pbank[:, hh * 512:(hh + 1) * 512],
                                        w[:, m, h * 128:(h + 1) * 128],
                                        self.nT[:, m, :],
                                        start=(m == 0), stop=(m == DT - 1))
                s_pe.inc(ins)

            @self.block.scalar
            def _(sc, idx=idx, dst=dst, rnd=rnd, pbank=pbank, pname=pname):
                s_pe.wait(sc)
                self.bw(sc, ifp)
                self.bw(sc, ifd)
                if idx == 0:
                    self.load_act(sc, "silu_and_others")
                dd = dst[:, tp, rnd * 2:(rnd + 1) * 2, :].rearrange("p a b -> p (a b)")
                s_ac.inc(sc.activation(dd, pbank[:, :], AF.Silu))
                self.bf_(pname, s_ac)

            @self.block.vector
            def _(v, idx=idx, dst=dst, rnd=rnd):
                s_ac.wait(v)
                self.bw(v, "sqs")
                dd = dst[:, tp, rnd * 2:(rnd + 1) * 2, :].rearrange("p a b -> p (a b)")
                s_dv.inc(v.tensor_tensor(self.sqs[:, idx * 1024:(idx + 1) * 1024], dd, dd, OP.mult))

        for s in range(NCH):
            pbank = self.p0 if s % 2 == 0 else self.p1
            pname = "p0" if s % 2 == 0 else "p1"

            @self.block.tensor
            def _(te, s=s, pbank=pbank, pname=pname):
                self.bw(te, pname)
                ins = None
                for m in range(DT):
                    ins = te.matmul(pbank[:, :], self.nT[:, m, s * 128:(s + 1) * 128],
                                    self.wv[:, m, :], start=(m == 0), stop=(m == DT - 1))
                s_pe.inc(ins)

            @self.block.scalar
            def _(sc, s=s, pbank=pbank, pname=pname):
                s_pe.wait(sc)
                self.bw(sc, ifp)
                self.bw(sc, ifd)
                s_ac.inc(sc.activation(self.vt[:, tp, s, :], pbank[:, :], AF.Silu))
                self.bf_(pname, s_ac)
                if s == NCH - 1:
                    self.load_act(sc, "natural_log_exp_and_others")

    def _l2ab(self, t):
        s_sp, s_tl, s_pe, s_dv, s_ac = self.s_sp, self.s_tl, self.s_pe, self.s_dv, self.s_ac
        tp = t % 2
        ifp, ifd = f"ifpe{tp}", f"ifdv{tp}"

        for idx, (dst, rnd) in enumerate([(self.qh, 0), (self.qh, 1), (self.kh, 0), (self.kh, 1)]):
            for hh in range(2):
                @self.block.tensor
                def _(te, idx=idx, hh=hh):
                    s_dv.wait(te)
                    self.bw(te, "p2")
                    s_pe.inc(te.matmul(self.p2[0:1, :], self.onescol[:, 0:1],
                                       self.sqs[:, idx * 1024 + hh * 512: idx * 1024 + (hh + 1) * 512],
                                       start=True, stop=True))

                @self.block.scalar
                def _(sc, idx=idx, hh=hh):
                    s_pe.wait(sc)
                    self.vchain(sc, s_ac, [
                        lambda: sc.activation(self.rowA[0:1, :], self.p2[0:1, :], AF.Ln,
                                              bias=self.epscol[0:1, 1:2]),
                        lambda: sc.activation(self.rowB[0:1, :], self.rowA[0:1, :], AF.Exp, scale=-0.5),
                    ])
                    self.bf_("p2", s_ac)

                @self.block.tensor
                def _(te, hh=hh):
                    s_ac.wait(te)
                    self.bw(te, "p3")
                    s_pe.inc(te.matmul(self.p3[:, :], self.onesrowb[0:1, :], self.rowB[0:1, :],
                                       start=True, stop=True))

                @self.block.vector
                def _(v, dst=dst, rnd=rnd, hh=hh):
                    s_pe.wait(v)
                    self.bw(v, ifp)
                    self.bw(v, ifd)
                    h = rnd * 2 + hh
                    s_dv.inc(v.tensor_tensor(dst[:, tp, h, :], dst[:, tp, h, :], self.p3[:, :], OP.mult))
                    self.bf_("p3", s_dv)

        @self.block.gpsimd
        def _(g):
            s_dv.wait(g)
            for h in range(HG):
                s_sp.inc(g.dma_start(out=self.d_kfm[t, h], in_=self.kh[:, tp, h, :]), 16)

        @self.block.sync
        def _(sy):
            s_sp.wait(sy)
            self.bw(sy, ifp)
            for h in range(HG):
                s_tl.inc(sy.dma_start(out=self.ktm[:, tp, h, :, :], in_=self.d_kfm[t, h], transpose=True), 16)

        @self.block.tensor
        def _(te):
            self.bw(te, "p2")
            for m in range(DT):
                te.matmul(self.p2[0:4, :], self.wab[:, m, 0:HG], self.nT[:, m, :],
                          start=(m == 0), stop=(m == DT - 1))
            self.bw(te, "p3")
            ins = None
            for m in range(DT):
                ins = te.matmul(self.p3[0:4, :], self.wab[:, m, HG:2 * HG], self.nT[:, m, :],
                                start=(m == 0), stop=(m == DT - 1))
            s_pe.inc(ins)

        @self.block.scalar
        def _(sc):
            s_pe.wait(sc)
            self.bw(sc, ifp)
            self.bw(sc, ifd)
            sc.activation(self.pa[0:4, tp, :], self.p2[0:4, :], AF.Exp, scale=-1.0)
            s_ac.inc(sc.activation(self.pbr[0:4, tp, :], self.p3[0:4, :], AF.Exp, scale=-1.0))
            self.bf_("p2", s_ac)
            self.bf_("p3", s_ac)

        @self.block.vector
        def _(v):
            s_ac.wait(v)
            v.tensor_scalar(self.cmb[0:4, tp, :], self.pa[0:4, tp, :], 1.0, None, OP.add)
            s_dv.inc(v.tensor_scalar(self.sfx[0:4, tp, :], self.pbr[0:4, tp, :], 1.0, None, OP.add))

        @self.block.scalar
        def _(sc):
            s_dv.wait(sc)
            sc.activation(self.pa[0:4, tp, :], self.cmb[0:4, tp, :], AF.Ln)
            s_ac.inc(sc.activation(self.pbr[0:4, tp, :], self.sfx[0:4, tp, :], AF.Ln))

        @self.block.vector
        def _(v):
            s_ac.wait(v)
            for g_ in range(NCH):
                sl = slice(g_ * 128, (g_ + 1) * 128)
                ins = v.tensor_tensor_scan(self.pcr[0:4, tp, sl], self.pa[0:4, tp, sl],
                                           self.pa[0:4, tp, sl], 0.0, OP.add, OP.bypass)
            s_dv.inc(ins)
            v.wait_ge(s_dv.h, s_dv.n)
            for g_ in range(NCH):
                sl = slice(g_ * 128, (g_ + 1) * 128)
                ins = v.tensor_scalar(self.sfx[0:4, tp, sl], self.pcr[0:4, tp, sl],
                                      self.pcr[0:4, tp, g_ * 128 + 127:g_ * 128 + 128], -1.0,
                                      OP.subtract, OP.mult)
            s_dv.inc(ins)
            v.wait_ge(s_dv.h, s_dv.n)
            s_dv.inc(v.scalar_tensor_tensor(self.cmb[0:4, tp, :], self.pcr[0:4, tp, :], -1.0,
                                            self.pbr[0:4, tp, :], OP.mult, OP.subtract))

        for g_ in range(NCH):
            @self.block.tensor
            def _(te, g_=g_):
                s_dv.wait(te)
                self.bw(te, "p3")
                sl = slice(g_ * 128, (g_ + 1) * 128)
                te.transpose(self.p3[:, 0:4], self.pcr[0:4, tp, sl], self.identf[0:4, 0:4])
                te.transpose(self.p3[:, 4:8], self.pbr[0:4, tp, sl], self.identf[0:4, 0:4])
                te.transpose(self.p3[:, 8:12], self.sfx[0:4, tp, sl], self.identf[0:4, 0:4])
                s_pe.inc(te.transpose(self.p3[:, 12:16], self.cmb[0:4, tp, sl], self.identf[0:4, 0:4]))

            @self.block.vector
            def _(v, g_=g_):
                s_pe.wait(v)
                s_dv.inc(v.tensor_copy(self.ccol[:, tp, g_, 0:16], self.p3[:, 0:16]))
                self.bf_("p3", s_dv)

            @self.block.vector
            def _(v, g_=g_):
                s_dv.wait(v)
                s_dv.inc(v.tensor_tensor(self.dcol[:, tp, g_, 24:28], self.ccol[:, tp, g_, 0:4],
                                         self.ccol[:, tp, g_, 8:12], OP.add))

            @self.block.scalar
            def _(sc, g_=g_):
                s_dv.wait(sc)
                sc.activation(self.dcol[:, tp, g_, 0:4], self.ccol[:, tp, g_, 4:8], AF.Exp, scale=-1.0)
                sc.activation(self.dcol[:, tp, g_, 4:8], self.ccol[:, tp, g_, 12:16], AF.Exp)
                sc.activation(self.dcol[:, tp, g_, 8:12], self.ccol[:, tp, g_, 0:4], AF.Exp, scale=-1.0)
                sc.activation(self.dcol[:, tp, g_, 12:16], self.ccol[:, tp, g_, 8:12], AF.Exp, scale=-1.0)
                s_ac.inc(sc.activation(self.dcol[:, tp, g_, 16:20], self.dcol[:, tp, g_, 24:28],
                                       AF.Exp, scale=-1.0))

        self.bf_("nT", s_pe)
        self.bf_("sqs", s_pe)
        self.bf_(f"acol{tp}", s_ac)
        self.fdv[t] = s_dv.n

    def _chunk_solve(self, t, g_):
        s_sp, s_tl, s_pe, s_dv, s_ac, s_pl = (self.s_sp, self.s_tl, self.s_pe,
                                              self.s_dv, self.s_ac, self.s_pl)
        sl = slice(g_ * 128, (g_ + 1) * 128)
        tp = t % 2
        tg = t * NCH + g_
        cp = tg % 2
        stages = []

        st = {}

        def stage_front():
            @self.block.vector
            def _(v):
                self.bw(v, "rm")
                v.tensor_tensor(self.rmp[0:4, :].rearrange("k (h j) -> k h j", h=HG),
                                self.pcr[0:4, tp, sl].unsqueeze(1).broadcast_to((4, HG, 128)),
                                self.hmask[0:4, :].rearrange("k (h j) -> k h j", h=HG), OP.mult)
                v.tensor_scalar(self.rmn[0:4, :], self.rmp[0:4, :], -1.0, None, OP.mult)
                s_dv.inc(v.tensor_tensor(self.rmc[0:4, :].rearrange("k (h j) -> k h j", h=HG),
                                         self.cmb[0:4, tp, sl].unsqueeze(1).broadcast_to((4, HG, 128)),
                                         self.hmask[0:4, :].rearrange("k (h j) -> k h j", h=HG), OP.mult))
            st["rm_dv"] = s_dv.n

            @self.block.tensor
            def _(te):
                te.wait_ge(s_dv.h, self.fdv[t])
                self.bw(te, "p0")
                for h in range(HG):
                    te.matmul(self.p0[:, h * 128:(h + 1) * 128], self.kh[:, tp, h, sl],
                              self.kh[:, tp, h, sl], start=True, stop=True)
                self.bw(te, "p1")
                ins = None
                for h in range(HG):
                    ins = te.matmul(self.p1[:, h * 128:(h + 1) * 128], self.kh[:, tp, h, sl],
                                    self.qh[:, tp, h, sl], start=True, stop=True)
                s_pe.inc(ins)

        def build_mat(rhs_rows, colap, pmat, mask, out, gguard=None):
            @self.block.tensor
            def _(te):
                te.wait_ge(s_dv.h, st["rm_dv"])
                self.bw(te, "p2")
                s_pe.inc(te.matmul(self.p2[:, :], self.ones4[0:4, :], rhs_rows, start=True, stop=True))
            pe0 = s_pe.n

            @self.block.vector
            def _(v):
                v.wait_ge(s_pe.h, pe0)
                ins = None
                for h in range(HG):
                    ins = v.tensor_scalar(self.argf[:, h * 128:(h + 1) * 128],
                                          self.p2[:, h * 128:(h + 1) * 128],
                                          colap[:, h:h + 1], 0.0, OP.add, OP.min)
                s_dv.inc(ins)
                self.bf_("p2", s_dv)
            dv1 = s_dv.n

            @self.block.scalar
            def _(sc):
                sc.wait_ge(s_dv.h, dv1)
                s_ac.inc(sc.activation(self.eM[:, :], self.argf[:, :], AF.Exp))
            ac1 = s_ac.n

            @self.block.vector
            def _(v):
                v.wait_ge(s_ac.h, ac1)
                self.bw(v, "tpb")
                s_dv.inc(v.tensor_tensor(self.tp[:, :], self.eM[:, :], pmat[:, :], OP.mult))
            dv2 = s_dv.n

            @self.block.gpsimd
            def _(g):
                g.wait_ge(s_dv.h, dv2)
                if gguard is not None:
                    self.bw(g, gguard)
                s_pl.inc(g.tensor_tensor(out[:, :].rearrange("p (h j) -> p h j", h=HG),
                                         self.tp[:, :].rearrange("p (h j) -> p h j", h=HG),
                                         mask[:, :].unsqueeze(1).broadcast_to((128, HG, 128)),
                                         OP.mult))
            self.bf_("tpb", s_pl)

        def stage_bm0():
            build_mat(self.rmp[0:4, :], self.ccol[:, tp, g_, 12:16], self.p0, self.mls, self.Mx,
                      gguard="MxF")

        def stage_bm1():
            build_mat(self.rmc[0:4, :], self.ccol[:, tp, g_, 0:4], self.p0, self.mus, self.MTx)
            build_mat(self.rmn[0:4, :], self.ccol[:, tp, g_, 0:4], self.p1, self.mui,
                      self.NTx[:, cp], gguard=f"NTx{cp}")
            self.bf_("rm", s_pl)
            self.bf_("p0", s_dv)
            self.bf_("p1", s_dv)
            st["bm_pl"] = s_pl.n

        PwA, PwB = self.Pw[0], self.Pw[1]
        PwT = self.Pw[2:7]
        PwF = self.Mx
        chains = [(self.Mx, self.MTx, PwA, PwT[0]), (PwA, PwT[0], PwB, PwT[1]),
                  (PwB, PwT[1], PwA, PwT[2]), (PwA, PwT[2], PwB, PwT[3]),
                  (PwB, PwT[3], PwA, PwT[4])]

        def mk_sq(k, P_, PT, oP, oPT):
            def stage():
                @self.block.tensor
                def _(te):
                    if k == 0:
                        te.wait_ge(s_pl.h, st["bm_pl"])
                    else:
                        te.wait_ge(s_dv.h, st[f"sq{k - 1}_dv"])
                        te.wait_ge(s_ac.h, st[f"sq{k - 1}_ac"])
                    self.bw(te, "p3")
                    for h in range(HG):
                        hs = slice(h * 128, (h + 1) * 128)
                        te.matmul(self.p3[:, hs], PT[:, hs], P_[:, hs], start=True, stop=True)
                    self.bw(te, "pb0h")
                    self.bw(te, "pb0")
                    ins = None
                    for h in range(HG):
                        hs = slice(h * 128, (h + 1) * 128)
                        ins = te.matmul(self.pb0[:, 512:1024][:, hs], P_[:, hs], PT[:, hs],
                                        start=True, stop=True)
                    s_pe.inc(ins)
                pe1 = s_pe.n

                @self.block.vector
                def _(v):
                    v.wait_ge(s_pe.h, pe1)
                    s_dv.inc(v.tensor_copy(oP[:, :], self.p3[:, :]))
                    self.bf_("p3", s_dv)
                st[f"sq{k}_dv"] = s_dv.n

                @self.block.scalar
                def _(sc):
                    sc.wait_ge(s_pe.h, pe1)
                    s_ac.inc(sc.copy(oPT[:, :], self.pb0[:, 512:1024]))
                    self.bf_("pb0h", s_ac)
                st[f"sq{k}_ac"] = s_ac.n
            return stage

        def stage_r0():
            @self.block.vector
            def _(v):
                s_tl.wait(v)
                self.bw(v, f"acol{tp}")
                self.bw(v, "Rab")
                ins = None
                for h in range(HG):
                    v.tensor_scalar(self.Ra[:, h, 0:128], self.vt[:, tp, g_, h * 128:(h + 1) * 128],
                                    self.dcol[:, tp, g_, 0 + h:1 + h], None, OP.mult)
                    ins = v.tensor_scalar(self.Ra[:, h, 128:256], self.ktm[:, tp, h, g_, :],
                                          self.dcol[:, tp, g_, 4 + h:5 + h], None, OP.mult)
                s_dv.inc(ins)
            st["ap-1_dep"] = (s_dv, s_dv.n)

        def stage_fin():
            @self.block.tensor
            def _(te):
                te.wait_ge(s_dv.h, st["sq4_dv"])
                te.wait_ge(s_ac.h, st["sq4_ac"])
                self.bw(te, "p3")
                ins = None
                for h in range(HG):
                    hs = slice(h * 128, (h + 1) * 128)
                    ins = te.matmul(self.p3[:, hs], PwA[:, hs], PwT[4][:, hs], start=True, stop=True)
                s_pe.inc(ins)
            pef = s_pe.n

            @self.block.vector
            def _(v):
                v.wait_ge(s_pe.h, pef)
                s_dv.inc(v.tensor_copy(PwF[:, :], self.p3[:, :]))
                self.bf_("p3", s_dv)
            st["fin_dv"] = s_dv.n

        seq = [(self.MTx, -1.0), (PwT[0], 1.0), (PwT[1], 1.0), (PwT[2], 1.0),
               (PwT[3], 1.0), (PwT[4], 1.0), (PwF, 1.0)]

        def mk_apply(i, lhsT, sgn, cur, nxt):
            def stage():
                @self.block.tensor
                def _(te):
                    dep = st[f"ap{i - 1}_dep"]
                    te.wait_ge(dep[0].h, dep[1])
                    if i == 0:
                        te.wait_ge(s_pl.h, st["bm_pl"])
                    elif i <= 5:
                        te.wait_ge(s_ac.h, st[f"sq{i - 1}_ac"])
                    else:
                        te.wait_ge(s_dv.h, st["fin_dv"])
                    self.bw(te, "pb1")
                    ins = None
                    for h in range(HG):
                        hs = slice(h * 128, (h + 1) * 128)
                        ins = te.matmul(self.pb1[:, h * 256:(h + 1) * 256], lhsT[:, hs], cur[:, h, :],
                                        start=True, stop=True)
                    s_pe.inc(ins)
                pea = s_pe.n

                @self.block.vector
                def _(v):
                    v.wait_ge(s_pe.h, pea)
                    s_dv.inc(v.scalar_tensor_tensor(nxt[:, :, :],
                                                    self.pb1[:, :].rearrange("p (h j) -> p h j", h=HG),
                                                    sgn, cur[:, :, :],
                                                    OP.mult, OP.add))
                    self.bf_("pb1", s_dv)
                st[f"ap{i}_dep"] = (s_dv, s_dv.n)
            return stage

        apply_stages = []
        cur, nxt = self.Ra, self.Rb
        for i, (lhsT, sgn) in enumerate(seq):
            apply_stages.append(mk_apply(i, lhsT, sgn, cur, nxt))
            cur, nxt = nxt, cur
        Rfin = cur

        def stage_wmt():
            @self.block.tensor
            def _(te):
                dep = st["ap6_dep"]
                te.wait_ge(dep[0].h, dep[1])
                self.bw(te, "p2")
                ins = None
                for h in range(HG):
                    ins = te.matmul(self.p2[:, h * 128:(h + 1) * 128], Rfin[:, h, 128:256],
                                    self.ident[:, :], start=True, stop=True)
                s_pe.inc(ins)
            pew = s_pe.n

            @self.block.vector
            def _(v):
                v.wait_ge(s_pe.h, pew)
                self.bw(v, f"WmT{cp}")
                self.bw(v, f"Uf{cp}")
                v.tensor_copy(self.WmT[:, cp].rearrange("p h c -> p (h c)"), self.p2[:, :])
                s_dv.inc(v.tensor_copy(self.Ufree[:, cp], Rfin[:, :, 0:128]))
                self.bf_("p2", s_dv)
                self.bf_("Rab", s_dv)
            self.bf_("MxF", s_pe)
            self.sdv_solve[tg] = s_dv.n

        sq = [mk_sq(k, *c) for k, c in enumerate(chains)]
        ap = apply_stages
        stages = [stage_front, stage_bm0, stage_bm1, stage_r0,
                  sq[0], ap[0], sq[1], ap[1], sq[2], ap[2], sq[3], ap[3], sq[4], ap[4],
                  stage_fin, ap[5], ap[6], stage_wmt]
        return stages

    def _chunk_chain(self, t, g_):
        s_sp, s_pe, s_dv = self.s_sp, self.s_pe, self.s_dv
        sl = slice(g_ * 128, (g_ + 1) * 128)
        tp = t % 2
        tg = t * NCH + g_
        cp = tg % 2
        Scur = self.Sst[tg % 2]
        Snew = self.Sst[(tg + 1) % 2]
        st = {}

        def c0():
            @self.block.tensor
            def _(te):
                te.wait_ge(s_dv.h, max(self.sdv_solve[tg], self.sdv_snew.get(tg - 1, 0)))
                self.bw(te, "pb0")
                self.bw(te, "pb0c")
                ins = None
                for h in range(HG):
                    ins = te.matmul(self.pb0[:, h * 128:(h + 1) * 128], self.WmT[:, cp, h, :],
                                    Scur[:, h, :], start=True, stop=True)
                s_pe.inc(ins)
            self.bf_(f"WmT{cp}", s_pe)
            st["c0"] = s_pe.n

        def c1():
            @self.block.vector
            def _(v):
                v.wait_ge(s_pe.h, st["c0"])
                v.scalar_tensor_tensor(self.Uu[:, :, :],
                                       self.pb0[:, 0:512].rearrange("p (h j) -> p h j", h=HG),
                                       -1.0, self.Ufree[:, cp],
                                       OP.mult, OP.add)
                ins = None
                for h in range(HG):
                    ins = v.tensor_scalar(self.Ug[:, h, :], self.Uu[:, h, :],
                                          self.dcol[:, tp, g_, 12 + h:13 + h], None, OP.mult)
                s_dv.inc(ins)
                self.bf_("pb0c", s_dv)
                self.bf_(f"Uf{cp}", s_dv)
            st["c1"] = s_dv.n

        def c2():
            @self.block.tensor
            def _(te):
                te.wait_ge(s_dv.h, st["c1"])
                self.bw(te, "p3")
                ins = None
                for h in range(HG):
                    ins = te.matmul(self.p3[:, h * 128:(h + 1) * 128], self.ktm[:, tp, h, g_, :],
                                    self.Ug[:, h, :], start=True, stop=True)
                s_pe.inc(ins)
            st["c2"] = s_pe.n

        def c3():
            @self.block.vector
            def _(v):
                v.wait_ge(s_pe.h, st["c2"])
                ins = None
                for h in range(HG):
                    ins = v.scalar_tensor_tensor(Snew[:, h, :], Scur[:, h, :],
                                                 self.dcol[:, tp, g_, 16 + h:17 + h],
                                                 self.p3[:, h * 128:(h + 1) * 128], OP.mult, OP.add)
                s_dv.inc(ins)
                self.bf_("p3", s_dv)
            self.sdv_snew[tg] = s_dv.n

        def c4():
            @self.block.tensor
            def _(te):
                te.wait_ge(s_dv.h, self.sdv_snew[tg])
                self.bw(te, "p0")
                for h in range(HG):
                    te.matmul(self.p0[:, h * 128:(h + 1) * 128],
                              self.NTx[:, cp, h * 128:(h + 1) * 128],
                              self.Uu[:, h, :], start=True, stop=True)
                self.bw(te, "p1")
                ins = None
                for h in range(HG):
                    ins = te.matmul(self.p1[:, h * 128:(h + 1) * 128], self.qh[:, tp, h, sl],
                                    Scur[:, h, :], start=True, stop=True)
                s_pe.inc(ins)
            self.bf_(f"NTx{cp}", s_pe)
            st["c4"] = s_pe.n

            @self.block.vector
            def _(v):
                v.wait_ge(s_pe.h, st["c4"])
                self.bw(v, "Ox")
                for h in range(HG):
                    v.tensor_scalar(self.t_o[:, h * 128:(h + 1) * 128],
                                    self.p1[:, h * 128:(h + 1) * 128],
                                    self.dcol[:, tp, g_, 8 + h:9 + h], None, OP.mult)
                s_dv.inc(v.tensor_tensor(self.Ox[:, :], self.t_o[:, :], self.p0[:, :], OP.add))
                self.bf_("p0", s_dv)
                self.bf_("p1", s_dv)

            @self.block.gpsimd
            def _(g):
                s_dv.wait(g)
                r = t * TILE + g_ * 128
                s_sp.inc(g.dma_start(out=self.d_rec[r:r + 128, :], in_=self.Ox[:, :]), 16)
            self.bf_("Ox", s_sp)
        return [c0, c1, c2, c3, c4]

    def _exchange_half(self):
        s_sp, s_cc = self.s_sp, self.s_cc
        half = self.nta * TILE // 2
        spv = s_sp.n

        @self.block.gpsimd
        def _(g):
            g.wait_ge(s_sp.h, spv)
            ins = g.collective_compute(
                "AllGather", mybir.AluOpType.bypass,
                replica_groups=[[0, 1], [2, 3], [4, 5], [6, 7]],
                ins=[self.d_rec[0:half, :].opt()],
                outs=[self.d_ag[0:2 * half, :].opt()],
            )
            s_cc.inc(ins, 1)

    def _exchange(self):
        s_sp, s_cc = self.s_sp, self.s_cc
        half = self.nta * TILE // 2

        @self.block.gpsimd
        def _(g):
            s_sp.wait(g)
            ins = g.collective_compute(
                "AllGather", mybir.AluOpType.bypass,
                replica_groups=[[0, 1], [2, 3], [4, 5], [6, 7]],
                ins=[self.d_rec[half:2 * half, :].opt()],
                outs=[self.d_ag[2 * half:4 * half, :].opt()],
            )
            s_cc.inc(ins, 1)
            g.wait_ge(s_cc.h, s_cc.n)
            reg = g.alloc_register()
            reg2 = g.alloc_register()
            g.load(reg, self.d_poff[0:1, 0:1])
            g.load(reg2, self.d_poff2[0:1, 0:1])
            v1 = g.snap(reg)
            v2 = g.snap(reg2)
            nb = self.ntb * TILE
            s_cc.inc(g.dma_start(out=self.d_oown[:, 0:HG * KD],
                                 in_=self.d_ag[bass.ds(v1, nb), :]), 16)
            s_cc.inc(g.dma_start(out=self.d_oown[:, HG * KD:D],
                                 in_=self.d_ag[bass.ds(v1 + half, nb), :]), 16)
            s_cc.inc(g.dma_start(out=self.d_xnown[:, :],
                                 in_=self.d_xn[bass.ds(v2, nb), :]), 16)
            g.wait_ge(s_cc.h, s_cc.n)

        @self.block.sync
        def _(sy):
            self.bw(sy, "swb_done")
            for i in range(DT):
                self.s_tl.inc(sy.dma_start(out=self.swt[:, :, i * 128:(i + 1) * 128],
                                           in_=self.d_swb[i * 128:(i + 1) * 128, :], transpose=True), 16)
        self.tl_swt = self.s_tl.n

    def _loadB(self, t):
        s_cc = self.s_cc
        slot = t % 2
        sem = self.s_ta[slot]
        r0 = t * TILE

        @self.block.scalar
        def _(sc, t=t, slot=slot, r0=r0, sem=sem):
            if t == 0:
                s_cc.wait(sc)
            self.bw(sc, f"oTs{slot}")
            sem.inc(sc.dma_start(out=self.oT[:, slot], in_=self.d_oown[r0:r0 + TILE, :],
                                 transpose=True), 16)
            self.bw(sc, f"xnP{slot}")
            self.bw(sc, f"xnV{slot}")
            sem.inc(sc.dma_start(out=self.xnT[:, slot], in_=self.d_xnown[r0:r0 + TILE, :],
                                 transpose=True), 16)
        self.tlB[t] = sem.n

    def _w1load(self, t, fb):
        wslot = fb % 2
        sem = self.s_w1[wslot]

        @self.block.sync
        def _(sy, fb=fb, wslot=wslot, sem=sem):
            sy.wait_ge(self.s_wc.h, 256)
            self.bw(sy, f"w1s{wslot}")
            sem.inc(sy.dma_start(out=self.w1s[:, wslot], in_=self.d_w1b[fb]), 16)
        self.wbv[(t, "w1", fb)] = sem.n

    def _w2load(self, t, jd):
        wslot = jd % 2
        sem = self.s_w2[wslot]

        @self.block.sync
        def _(sy, jd=jd, wslot=wslot, sem=sem):
            sy.wait_ge(self.s_wc.h, 256)
            self.bw(sy, f"w2s{wslot}")
            sem.inc(sy.dma_start(out=self.w2s[:, wslot], in_=self.d_w2b[jd]), 16)
        self.wbv[(t, "w2", jd)] = sem.n

    def _tileB(self, t):
        s_x, s_tl, s_pe, s_dv, s_ac, s_pl, s_wb = (self.s_x, self.s_tl, self.s_pe, self.s_dv,
                                                   self.s_ac, self.s_pl, self.s_wb)
        r0 = t * TILE
        slot = t % 2
        if t + 1 < self.ntb:
            self._loadB(t + 1)
        self._w1load(t, 0)
        self._w1load(t, 1)

        for jd in range(DT):
            pbank = self.p0 if jd % 2 == 0 else self.p1
            pname = "p0" if jd % 2 == 0 else "p1"

            @self.block.tensor
            def _(te, jd=jd, pbank=pbank, pname=pname):
                te.wait_ge(self.s_ta[slot].h, self.tlB[t])
                self.s_ld.wait(te, self.ld_essential)
                self.bw(te, pname)
                ins = None
                for m in range(DT):
                    ins = te.matmul(pbank[:, :], self.wo[:, m, jd * 128:(jd + 1) * 128],
                                    self.oT[:, slot, m, :], start=(m == 0), stop=(m == DT - 1))
                s_pe.inc(ins)

            @self.block.vector
            def _(v, jd=jd, pbank=pbank, pname=pname):
                s_pe.wait(v)
                self.bw(v, "ybfT")
                s_dv.inc(v.tensor_tensor(self.ybf[:, jd, :], pbank[:, :],
                                         self.xnT[:, slot, jd, :], OP.add))
                self.bf_(pname, s_dv)
        self.bf_(f"oTs{slot}", s_pe)
        self.bf_(f"xnV{slot}", s_dv)

        @self.block.vector
        def _(v):
            s_dv.wait(v)
            self.bw(v, "hgx")
            s_dv.inc(v.tensor_tensor(self.hgx[:, 0:DT, :].rearrange("p m c -> p (m c)"),
                                     self.ybf[:, :, :].rearrange("p m c -> p (m c)"),
                                     self.ybf[:, :, :].rearrange("p m c -> p (m c)"), OP.mult))

        @self.block.tensor
        def _(te):
            s_dv.wait(te)
            self.bw(te, "p2")
            for m in range(DT):
                te.matmul(self.p2[0:1, :], self.onescol[:, 0:1], self.ybf[:, m, :],
                          start=(m == 0), stop=(m == DT - 1))
            self.bw(te, "p3")
            ins = None
            for m in range(DT):
                ins = te.matmul(self.p3[0:1, :], self.onescol[:, 0:1], self.hgx[:, m, :],
                                start=(m == 0), stop=(m == DT - 1))
            s_pe.inc(ins)
        self.bf_("ybfT", s_pe)
        stat_pe = s_pe.n

        for jo in range(DT):
            half = jo % 2

            @self.block.tensor
            def _(te, jo=jo, half=half):
                if jo == 0:
                    self.bw(te, "pb0")
                    self.bw(te, "pb0c")
                    te.wait_ge(s_tl.h, self.tl_swt)
                self.bw(te, f"gb{half}")
                ins = None
                for m in range(DT):
                    ins = te.matmul(self.pb0[:, half * 512:(half + 1) * 512],
                                    self.swt[:, m, jo * 128:(jo + 1) * 128],
                                    self.xnT[:, slot, m, :], start=(m == 0), stop=(m == DT - 1))
                s_pe.inc(ins)

            @self.block.scalar
            def _(sc, jo=jo, half=half):
                s_pe.wait(sc)
                self.bw(sc, "tt1")
                if jo == 0 and t == 0:
                    self.load_act(sc, "gelu_and_others")
                s_ac.inc(sc.activation(self.tt1[:, :], self.pb0[:, half * 512:(half + 1) * 512],
                                       AF.Tanh, scale=0.5))

            @self.block.vector
            def _(v, jo=jo, half=half):
                s_ac.wait(v)
                self.bw(v, "gsb")
                self.bw(v, "tt2g")
                v.tensor_scalar(self.tt2[:, :], self.tt1[:, :], 0.5, 0.5, OP.mult, OP.add)
                s_dv.inc(v.tensor_tensor(self.gsb[:, jo, :], self.tt2[:, :],
                                         self.pb0[:, half * 512:(half + 1) * 512], OP.mult))
                self.bf_(f"gb{half}", s_dv)
                self.bf_("tt1", s_dv)
        self.bf_(f"xnP{slot}", s_pe)

        @self.block.vector
        def _(v):
            v.wait_ge(s_pe.h, stat_pe)
            self.vchain(v, s_dv, [
                lambda: v.tensor_scalar(self.rw[0:1, 0, :], self.p2[0:1, :], 1.0 / D, None, OP.mult),
                lambda: v.tensor_scalar(self.rw[0:1, 1, :], self.p3[0:1, :], 1.0 / D, None, OP.mult),
                lambda: v.tensor_tensor(self.rw[0:1, 2, :], self.rw[0:1, 0, :], self.rw[0:1, 0, :], OP.mult),
                lambda: v.tensor_tensor(self.rw[0:1, 1, :], self.rw[0:1, 1, :], self.rw[0:1, 2, :], OP.subtract),
            ])
            self.bf_("p2", s_dv)
            self.bf_("p3", s_dv)

        @self.block.scalar
        def _(sc):
            s_dv.wait(sc)
            self.load_act(sc, "natural_log_exp_and_others")
            self.vchain(sc, s_ac, [
                lambda: sc.activation(self.rw[0:1, 2, :], self.rw[0:1, 1, :], AF.Ln,
                                      bias=self.epscol[0:1, 0:1]),
                lambda: sc.activation(self.rw[0:1, 1, :], self.rw[0:1, 2, :], AF.Exp, scale=-0.5),
            ])

        @self.block.vector
        def _(v):
            s_ac.wait(v)
            s_dv.inc(v.scalar_tensor_tensor(self.rw[0:1, 2, :], self.rw[0:1, 0, :], -1.0,
                                            self.rw[0:1, 1, :], OP.mult, OP.mult))

        @self.block.tensor
        def _(te):
            s_dv.wait(te)
            self.bw(te, "bc")
            if t == 0:
                self.bw(te, "pb1")
            te.matmul(self.pb1[:, 0:512], self.onesrowf[0:1, :], self.rw[0:1, 1, :],
                      start=True, stop=True)
            s_pe.inc(te.matmul(self.pb1[:, 512:1024], self.onesrowf[0:1, :], self.rw[0:1, 2, :],
                               start=True, stop=True))

        @self.block.vector
        def _(v):
            s_pe.wait(v)
            self.bw(v, "bcc")
            v.tensor_copy(self.bca[:, :], self.pb1[:, 0:512])
            s_dv.inc(v.tensor_copy(self.bcc[:, :], self.pb1[:, 512:1024]))
            self.bf_("bc", s_dv)

        for jd in range(DT):
            @self.block.vector
            def _(v, jd=jd):
                s_dv.wait(v)
                self.bw(v, "hT")
                s_dv.inc(v.tensor_tensor(self.tt3[:, :], self.ybf[:, jd, :], self.bca[:, :], OP.mult))

            @self.block.gpsimd
            def _(g, jd=jd):
                s_dv.wait(g)
                s_pl.inc(g.tensor_tensor(self.tt2[:, :], self.tt3[:, :], self.bcc[:, :], OP.add))

            @self.block.vector
            def _(v, jd=jd):
                s_pl.wait(v)
                s_dv.inc(v.tensor_scalar(self.hT[:, jd, :], self.tt2[:, :],
                                         self.ln2wc[:, jd:jd + 1], self.ln2bc[:, jd:jd + 1],
                                         OP.mult, OP.add))
        self.bf_("bcc", s_pl)

        for fb in range(8):
            fslot = fb % 2
            for i2 in range(4):
                ft = fb * 4 + i2
                pbank = self.p0 if ft % 2 == 0 else self.p1
                pname = "p0" if ft % 2 == 0 else "p1"

                @self.block.tensor
                def _(te, fb=fb, fslot=fslot, i2=i2, pbank=pbank, pname=pname):
                    te.wait_ge(self.s_w1[fslot].h, self.wbv[(t, "w1", fb)])
                    s_dv.wait(te)
                    self.bw(te, pname)
                    ins = None
                    for m in range(DT):
                        ins = te.matmul(pbank[:, :], self.w1s[:, fslot, m, i2 * 128:(i2 + 1) * 128],
                                        self.hT[:, m, :], start=(m == 0), stop=(m == DT - 1))
                    s_pe.inc(ins)

                @self.block.scalar
                def _(sc, ft=ft, pbank=pbank, pname=pname):
                    s_pe.wait(sc)
                    if ft == 0:
                        self.load_act(sc, "gelu_and_others")
                    s_ac.inc(sc.activation(self.hgx[:, ft, :], pbank[:, :], AF.Gelu,
                                           bias=self.b1c[:, ft:ft + 1]))
                    self.bf_(pname, s_ac)
            self.bf_(f"w1s{fslot}", self.s_pe)
            if fb + 2 < 8:
                self._w1load(t, fb + 2)
            if fb == 1:
                self._w2load(t, 0)
                self._w2load(t, 1)
        self.bf_("hT", self.s_pe)

        for jd in range(DT):
            pbank = self.p2 if jd % 2 == 0 else self.p3
            pname = "p2" if jd % 2 == 0 else "p3"
            wslot = jd % 2

            @self.block.tensor
            def _(te, jd=jd, pbank=pbank, pname=pname, wslot=wslot):
                te.wait_ge(self.s_w2[wslot].h, self.wbv[(t, "w2", jd)])
                s_ac.wait(te)
                self.bw(te, pname)
                ins = None
                for i in range(FT):
                    ins = te.matmul(pbank[:, :], self.w2s[:, wslot, i, :], self.hgx[:, i, :],
                                    start=(i == 0), stop=(i == FT - 1))
                s_pe.inc(ins)
            self.bf_(f"w2s{wslot}", self.s_pe)
            if jd + 2 < DT:
                self._w2load(t, jd + 2)

            @self.block.vector
            def _(v, jd=jd, pbank=pbank, pname=pname):
                s_pe.wait(v)
                v.scalar_tensor_tensor(self.tt3[:, :], pbank[:, :], 1.0, self.ybf[:, jd, :],
                                       OP.mult, OP.add)
                s_dv.inc(v.tensor_scalar(self.tt2[:, :], self.tt3[:, :],
                                         self.b2c[:, jd:jd + 1], None, OP.add))
                self.bf_(pname, s_dv)

            @self.block.gpsimd
            def _(g, jd=jd):
                s_dv.wait(g)
                s_pl.inc(g.tensor_tensor(self.gsb[:, jd, :], self.gsb[:, jd, :],
                                         self.tt2[:, :], OP.mult))
        self.bf_("hgx", self.s_pe)
        self.bf_("gsb", self.s_pl)
        self.bf_("tt2g", self.s_pl)

        @self.block.gpsimd
        def _(g):
            s_pl.wait(g)
            self.s_sp.inc(g.dma_start(out=self.d_sg[t].rearrange("(m p) c -> p m c", p=128),
                                      in_=self.gsb[:, :, :]), 16)

        @self.block.sync
        def _(sy):
            self.s_sp.wait(sy)
            self.bw(sy, "sgT")
            s_tl.inc(sy.dma_start(out=self.sgT[:, :, :], in_=self.d_sg[t], transpose=True), 16)
        tl_sg = s_tl.n

        for s in range(NCH):
            par = s % 2

            @self.block.sync
            def _(sy, s=s, par=par):
                self.bw(sy, f"xb{par}")
                s_x.inc(sy.dma_start(out=self.xb2[:, par, :],
                                     in_=self.d_xown[r0 + s * 128:r0 + (s + 1) * 128, :]), 16)

            @self.block.vector
            def _(v, s=s, par=par):
                s_x.wait(v)
                v.wait_ge(s_tl.h, tl_sg)
                self.bw(v, f"ob{par}")
                s_dv.inc(v.tensor_tensor(self.outb[:, par, :], self.xb2[:, par, :],
                                         self.sgT[:, s, :], OP.add))
                self.bf_(f"xb{par}", s_dv)

            @self.block.sync
            def _(sy, s=s, par=par):
                s_dv.wait(sy)
                self.s_st.inc(sy.dma_start(out=self.d_out[r0 + s * 128:r0 + (s + 1) * 128, :],
                                           in_=self.outb[:, par, :]), 16)
                self.bf_(f"ob{par}", self.s_st)
        self.bf_("sgT", s_dv)

    def _tapA(self):
        if not self.taps:
            return

        @self.block.sync
        def _(sy):
            self.s_dv.wait(sy)
            self.s_ac.wait(sy)
            self.s_pe.wait(sy)
            self.s_pl.wait(sy)
            self.s_sp.wait(sy)
            for (name, src, shape, dt) in self.taps:
                ap = src(self) if callable(src) else getattr(self, src)
                self.s_st.inc(sy.dma_start(out=self.d_taps[name][:, :] if len(shape) == 2
                                           else self.d_taps[name][:], in_=ap), 16)

    def _finish(self):
        if self.dbg_dram:
            nc = self.nc
            dbgs = [("dbg_rec", self.d_rec), ("dbg_ag", self.d_ag),
                    ("dbg_oown", self.d_oown), ("dbg_xnown", self.d_xnown),
                    ("dbg_xn", self.d_xn)]
            exts = {}
            for nm, srcap in dbgs:
                exts[nm] = nc.dram_tensor(nm, list(srcap.shape), srcap.dtype, kind="ExternalOutput").ap()

            @self.block.gpsimd
            def _(g):
                self.s_cc.wait(g)
                self.s_sp.wait(g)
                self.s_dv.wait(g)
                for nm, srcap in dbgs:
                    self.s_st.inc(g.dma_start(out=exts[nm][:, :], in_=srcap[:, :]), 16)

        if self.tapsB:
            @self.block.sync
            def _(sy):
                self.s_dv.wait(sy)
                self.s_ac.wait(sy)
                self.s_pe.wait(sy)
                self.s_pl.wait(sy)
                for (name, src, shape, dt) in self.tapsB:
                    ap = src(self) if callable(src) else getattr(self, src)
                    self.s_st.inc(sy.dma_start(out=self.d_taps[name][:, :] if len(shape) == 2
                                               else self.d_taps[name][:], in_=ap), 16)

        @self.block.sync
        def _(sy):
            self.s_st.wait(sy)
            self.s_sp.wait(sy)
            self.s_ld.wait(sy)


def _consts():
    ii = np.arange(128)
    c = {
        "ident_bf": np.eye(128, dtype=bfd),
        "ident_f32": np.eye(128, dtype=np.float32),
        "ones_col": np.ones((128, 1), dtype=bfd),
        "ones_row_bf": np.ones((1, 128), dtype=bfd),
        "ones_row_f32": np.ones((1, 128), dtype=np.float32),
        "ones4_f32": np.ones((4, 128), dtype=np.float32),
        "mask_ls": (ii[None, :] < ii[:, None]).astype(bfd),
        "mask_us": (ii[None, :] > ii[:, None]).astype(bfd),
        "mask_ui": (ii[None, :] >= ii[:, None]).astype(bfd),
    }
    hm = np.zeros((4, HG * 128), dtype=np.float32)
    for k in range(4):
        hm[k, k * 128:(k + 1) * 128] = 1.0
    c["hmask_f32"] = hm
    return c


def _run(x, Wq, Wk, Wv, Wa, Wb, Wo, ln1w, ln1b, ln2w, ln2b, W1, b1, W2, b2, ffnw):
    Tn = x.shape[1]
    nta = Tn // TILE
    half = Tn // 2
    consts = _consts()
    bld = Builder(nta=nta)
    nc = bld.build()
    in_maps = []
    for c in range(8):
        b, q = c // 2, c % 2
        hs = slice(q * HG * KD, (q + 1) * HG * KD)
        m = {
            "x_full": x[b],
            "x_own": np.ascontiguousarray(x[b, q * half:(q + 1) * half]),
            "wq": np.ascontiguousarray(Wq[:, hs]),
            "wk": np.ascontiguousarray(Wk[:, hs]),
            "wv": np.ascontiguousarray(Wv[:, hs]),
            "wab": np.ascontiguousarray(np.concatenate(
                [Wa[:, q * HG:(q + 1) * HG], Wb[:, q * HG:(q + 1) * HG]], axis=1)),
            "wo": Wo,
            "w1": W1,
            "w2": W2,
            "b1c": np.ascontiguousarray(b1.reshape(FT, 128).T),
            "b2c": np.ascontiguousarray(b2.reshape(DT, 128).T),
            "ln1w": ln1w[None, :],
            "ln1b": ln1b[None, :],
            "ln2wc": np.ascontiguousarray(ln2w.reshape(DT, 128).T),
            "ln2bc": np.ascontiguousarray(ln2b.reshape(DT, 128).T),
            "ffnw": ffnw,
            "poff": np.array([[q * Tn]], dtype=np.uint32),
            "poff2": np.array([[q * half]], dtype=np.uint32),
        }
        m.update(consts)
        in_maps.append(m)
    res = run_bass_kernel_spmd(nc, in_maps, core_ids=list(range(8)))
    out = np.empty((x.shape[0], Tn, D), dtype=np.float32)
    for c in range(8):
        b, q = c // 2, c % 2
        out[b, q * half:(q + 1) * half] = np.asarray(res.results[c]["out"], dtype=np.float32)
    return out


def kernel(L0_attn_x, L0_attn_Wq, L0_attn_Wk, L0_attn_Wv, L0_attn_Wb, L0_attn_Wa,
           L0_attn_Wo, L0_attn_ln1_w, L0_attn_ln1_b, L0_attn_ln2_w, L0_attn_ln2_b,
           L0_attn_W1, L0_attn_b1, L0_attn_W2, L0_attn_b2, L0_ffn_weight):
    x = np.ascontiguousarray(np.asarray(L0_attn_x, dtype=np.float32))
    Wq = np.asarray(L0_attn_Wq, dtype=np.float32)
    Wk = np.asarray(L0_attn_Wk, dtype=np.float32)
    Wv = np.asarray(L0_attn_Wv, dtype=np.float32)
    Wa = np.asarray(L0_attn_Wa, dtype=np.float32)
    Wb = np.asarray(L0_attn_Wb, dtype=np.float32)
    Wo = np.ascontiguousarray(np.asarray(L0_attn_Wo, dtype=np.float32))
    ln1w = np.asarray(L0_attn_ln1_w, dtype=np.float32)
    ln1b = np.asarray(L0_attn_ln1_b, dtype=np.float32)
    ln2w = np.asarray(L0_attn_ln2_w, dtype=np.float32)
    ln2b = np.asarray(L0_attn_ln2_b, dtype=np.float32)
    W1 = np.ascontiguousarray(np.asarray(L0_attn_W1, dtype=np.float32))
    b1 = np.asarray(L0_attn_b1, dtype=np.float32)
    W2 = np.ascontiguousarray(np.asarray(L0_attn_W2, dtype=np.float32))
    b2 = np.asarray(L0_attn_b2, dtype=np.float32)
    ffnw = np.ascontiguousarray(np.asarray(L0_ffn_weight, dtype=np.float32))

    return _run(x, Wq, Wk, Wv, Wa, Wb, Wo, ln1w, ln1b, ln2w, ln2b, W1, b1, W2, b2, ffnw)


# version 29
# speedup vs baseline: 1.0006x; 1.0006x over previous
import sys

for _p in ("/opt/trn_rl_repo",):
    if _p not in sys.path:
        sys.path.insert(0, _p)

import contextlib
import numpy as np
import ml_dtypes

import concourse.bass as bass
import concourse.mybir as mybir
from concourse.bass_utils import run_bass_kernel_spmd
from concourse.alu_op_type import AluOpType as OP

F32 = mybir.dt.float32
BF16 = mybir.dt.bfloat16
U32 = mybir.dt.uint32
AF = mybir.ActivationFunctionType
AX = mybir.AxisListType.X

B, T, D, H = 4, 8192, 1024, 8
HG = 4
KD = 128
F = 4096
TILE = 512
CH = 128
NCH = TILE // CH
DT = D // 128
FT = F // 128
EPS_LN = 1e-5
EPS_L2 = 1e-6
bfd = ml_dtypes.bfloat16


class Sem:
    def __init__(self, nc, es, name):
        self.h = es.enter_context(nc.semaphore(name))
        self.n = 0

    def inc(self, instr, k=1):
        instr.then_inc(self.h, k)
        self.n += k

    def wait(self, eng, n=None):
        eng.wait_ge(self.h, self.n if n is None else n)


class Builder:
    def __init__(self, nta=T // TILE, do_b=True, taps=(), tapsB=(), dbg_dram=False):
        self.nta = nta
        self.ntb = (nta * TILE // 2) // TILE if do_b else 0
        self.do_b = do_b
        self.taps = list(taps)
        self.tapsB = list(tapsB)
        self.dbg_dram = dbg_dram

    def build(self):
        nc = bass.Bass("TRN2", target_bir_lowering=False)
        self.nc = nc
        self._dram()
        es = contextlib.ExitStack()
        self.es = es
        with es:
            self._alloc_persistent()
            self.esA = contextlib.ExitStack()
            with self.esA:
                self._alloc_A()
                self.block = es.enter_context(nc.Block())
                self._phase0()
                self._sign_step(0)
                self.sdv_solve = {}
                self.sdv_snew = {}
                self.fdv = {}
                for f in self._front_pieces(0):
                    f()
                for f in self._chunk_solve(0, 0):
                    f()
                for t in range(self.nta):
                    pieces = self._front_pieces(t + 1) if t + 1 < self.nta else [None] * NCH
                    for g_ in range(NCH):
                        if pieces[g_] is not None:
                            pieces[g_]()
                        c = t * NCH + g_
                        if c + 1 < self.nta * NCH:
                            nt_, ng = divmod(c + 1, NCH)
                            SA = self._chunk_solve(nt_, ng)
                        else:
                            SA = []
                        CB = self._chunk_chain(t, g_)
                        order = list(SA[0:4])
                        k = 4
                        for cb in CB:
                            order.append(cb)
                            if k < len(SA):
                                order.append(SA[k])
                                k += 1
                        order += SA[k:]
                        for f in order:
                            f()
                    self.bf_(f"ifpe{t % 2}", self.s_pe)
                    self.bf_(f"ifdv{t % 2}", self.s_dv)
                    self._sign_step(t + 1)
                    if t < 3:
                        self._casts(t)
                    if self.do_b and t == self.nta // 2 - 1:
                        self._exchange_half()
                if self.do_b:
                    for t in range(self.nta + 1, DT + 1):
                        self._sign_step(t)
                    self._exchange()
                self._tapA()
            if self.do_b:
                self._alloc_B()
                self.tlB = {}
                self.wbv = {}
                self._loadB(0)
                for t in range(self.ntb):
                    self._tileB(t)
            self._finish()
        return nc

    def _dram(self):
        nc = self.nc
        dr = lambda n, s, d, k="ExternalInput": nc.dram_tensor(n, s, d, kind=k).ap()
        nta, ntb = self.nta, self.ntb
        self.d_x = dr("x_full", [nta * TILE, D], F32)
        self.d_wq = dr("wq", [D, HG * KD], F32)
        self.d_wk = dr("wk", [D, HG * KD], F32)
        self.d_wv = dr("wv", [D, HG * KD], F32)
        self.d_wab = dr("wab", [D, 2 * HG], F32)
        self.d_ln1w = dr("ln1w", [1, D], F32)
        self.d_ln1b = dr("ln1b", [1, D], F32)
        self.d_ident = dr("ident_bf", [128, 128], BF16)
        self.d_identf = dr("ident_f32", [128, 128], F32)
        self.d_onescol = dr("ones_col", [128, 1], BF16)
        self.d_onesrowb = dr("ones_row_bf", [1, 128], BF16)
        self.d_onesrowf = dr("ones_row_f32", [1, 128], F32)
        self.d_ones4 = dr("ones4_f32", [4, 128], F32)
        self.d_hmask = dr("hmask_f32", [4, HG * 128], F32)
        self.d_mask_ls = dr("mask_ls", [128, 128], BF16)
        self.d_mask_us = dr("mask_us", [128, 128], BF16)
        self.d_mask_ui = dr("mask_ui", [128, 128], BF16)
        idr = lambda n, s, d: nc.dram_tensor(n, s, d).ap()
        self.d_xn = idr("xn_spill", [nta * TILE, D], BF16)
        self.d_n = idr("n_spill", [nta * TILE, D], BF16)
        self.d_kfm = idr("kfm_spill", [nta, HG, 128, TILE], BF16)
        self.d_rec = idr("rec_spill", [nta * TILE, HG * KD], BF16)
        if self.do_b:
            self.d_xown = dr("x_own", [ntb * TILE, D], F32)
            self.d_wo = dr("wo", [D, D], F32)
            self.d_w1 = dr("w1", [D, F], F32)
            self.d_w2 = dr("w2", [F, D], F32)
            self.d_b1c = dr("b1c", [128, FT], F32)
            self.d_b2c = dr("b2c", [128, DT], F32)
            self.d_ln2wc = dr("ln2wc", [128, DT], F32)
            self.d_ln2bc = dr("ln2bc", [128, DT], F32)
            self.d_ffnw = dr("ffnw", [D, D], F32)
            self.d_poff = dr("poff", [1, 1], U32)
            self.d_poff2 = dr("poff2", [1, 1], U32)
            self.d_out = dr("out", [ntb * TILE, D], F32, k="ExternalOutput")
            self.d_ag = idr("ag_out", [2 * nta * TILE, HG * KD], BF16)
            self.d_oown = idr("o_own", [ntb * TILE, D], BF16)
            self.d_xnown = idr("xn_own", [ntb * TILE, D], BF16)
            self.d_w1b = idr("w1_bf", [8, 128, DT, 512], BF16)
            self.d_w2b = idr("w2_bf", [DT, 128, FT, 128], BF16)
            self.d_swb = idr("sw_bf", [D, D], BF16)
            self.d_sg = idr("sg_spill", [ntb, D, TILE], BF16)
        self.d_taps = {}
        for (name, _src, shape, dt) in list(self.taps) + list(self.tapsB):
            self.d_taps[name] = dr("tap_" + name, shape, dt, k="ExternalOutput")

    def _alloc_persistent(self):
        nc, es = self.nc, self.es
        sb = lambda n, s, d: es.enter_context(nc.sbuf_tensor(n, s, d))
        ps = lambda n, s: es.enter_context(nc.psum_tensor(n, s, F32))
        self.ident = sb("ident", [128, 128], BF16)
        self.identf = sb("identf", [128, 128], F32)
        self.onescol = sb("onescol", [128, 1], BF16)
        self.onesrowb = sb("onesrowb", [1, 128], BF16)
        self.onesrowf = sb("onesrowf", [1, 128], F32)
        self.ones4 = sb("ones4", [4, 128], F32)
        self.hmask = sb("hmask", [4, HG * 128], F32)
        self.mls = sb("mls", [128, 128], BF16)
        self.mus = sb("mus", [128, 128], BF16)
        self.mui = sb("mui", [128, 128], BF16)
        self.epscol = sb("epscol", [128, 2], F32)
        if self.do_b:
            self.wo = sb("wo_sb", [128, DT, D], BF16)
            self.swt = sb("swt_sb", [128, DT, D], BF16)
            self.b1c = sb("b1c_sb", [128, FT], F32)
            self.b2c = sb("b2c_sb", [128, DT], F32)
            self.ln2wc = sb("ln2wc_sb", [128, DT], F32)
            self.ln2bc = sb("ln2bc_sb", [128, DT], F32)
        self.pb0 = ps("pb0", [128, 1024])
        self.pb1 = ps("pb1", [128, 1024])
        self.p0 = ps("p0", [128, 512])
        self.p1 = ps("p1", [128, 512])
        self.p2 = ps("p2", [128, 512])
        self.p3 = ps("p3", [128, 512])
        S_ = lambda n: Sem(nc, es, n)
        self.s_lw = S_("s_lw")
        self.s_wc = S_("s_wc")
        self.s_sn = S_("s_sn")
        self.s_ta = [S_("s_ta0"), S_("s_ta1")]
        self.s_w1 = [S_("s_w1a"), S_("s_w1b")]
        self.s_w2 = [S_("s_w2a"), S_("s_w2b")]
        self.s_ld = S_("s_ld")
        self.s_x = S_("s_x")
        self.s_xp = [S_("s_xp0"), S_("s_xp1")]
        self.s_sp = S_("s_sp")
        self.s_tl = S_("s_tl")
        self.s_pe = S_("s_pe")
        self.s_dv = S_("s_dv")
        self.s_ac = S_("s_ac")
        self.s_pl = S_("s_pl")
        self.s_cc = S_("s_cc")
        self.s_st = S_("s_st")
        self.s_wb = S_("s_wb")
        self.guard = {}

    def _alloc_A(self):
        nc, es = self.nc, self.esA
        sb = lambda n, s, d: es.enter_context(nc.sbuf_tensor(n, s, d))
        self.wq = sb("wq_sb", [128, DT, HG * KD], BF16)
        self.wk = sb("wk_sb", [128, DT, HG * KD], BF16)
        self.wv = sb("wv_sb", [128, DT, HG * KD], BF16)
        self.wab = sb("wab_sb", [128, DT, 2 * HG], BF16)
        self.ln1w_bc = sb("ln1w_bc", [128, D], BF16)
        self.ln1b_bc = sb("ln1b_bc", [128, D], BF16)
        self.ln1row = sb("ln1row", [1, 2 * D], BF16)
        self.xs = sb("xs", [128, 2, D], F32)
        self.stat = sb("stat", [128, 32], F32)
        self.bnb = sb("bnb", [128, 2, 12], F32)
        self.xnb = sb("xnb", [128, NCH, D], BF16)
        self.nn = sb("nn", [128, NCH, D], BF16)
        self.nT = sb("nT", [128, DT, TILE], BF16)
        self.qh = sb("qh", [128, 2, HG, TILE], BF16)
        self.kh = sb("kh", [128, 2, HG, TILE], BF16)
        self.vt = sb("vt", [128, 2, NCH, HG * KD], BF16)
        self.ktm = sb("ktm", [128, 2, HG, NCH, 128], BF16)
        self.sqs = sb("sqs", [128, 8 * TILE], BF16)
        self.rowA = sb("rowA", [1, TILE], F32)
        self.rowB = sb("rowB", [1, TILE], BF16)
        self.pa = sb("pa", [4, 2, TILE], F32)
        self.pbr = sb("pbr", [4, 2, TILE], F32)
        self.pcr = sb("pcr", [4, 2, TILE], F32)
        self.sfx = sb("sfx", [4, 2, TILE], F32)
        self.cmb = sb("cmb", [4, 2, TILE], F32)
        self.rmp = sb("rmp", [4, HG * 128], F32)
        self.rmn = sb("rmn", [4, HG * 128], F32)
        self.rmc = sb("rmc", [4, HG * 128], F32)
        self.ccol = sb("ccol", [128, 2, NCH, 16], F32)
        self.dcol = sb("dcol", [128, 2, NCH, 28], F32)
        self.argf = sb("argf", [128, HG * 128], F32)
        self.eM = sb("eM", [128, HG * 128], BF16)
        self.tp = sb("tp", [128, HG * 128], BF16)
        self.Mx = sb("Mx", [128, HG * 128], BF16)
        self.MTx = sb("MTx", [128, HG * 128], BF16)
        self.NTx = sb("NTx", [128, 2, HG * 128], BF16)
        self.Pw = [sb(f"Pw{i}", [128, HG * 128], BF16) for i in range(7)]
        self.Ra = sb("Ra", [128, HG, 256], BF16)
        self.Rb = sb("Rb", [128, HG, 256], BF16)
        self.WmT = sb("WmT", [128, 2, HG, 128], BF16)
        self.Ufree = sb("Ufree", [128, 2, HG, 128], BF16)
        self.Ug = sb("Ug", [128, HG, 128], BF16)
        self.Uu = sb("Uu", [128, HG, 128], BF16)
        self.Sst = [sb("Sst0", [128, HG, 128], BF16), sb("Sst1", [128, HG, 128], BF16)]
        self.t_o = sb("t_o", [128, HG * 128], F32)
        self.Ox = sb("Ox", [128, HG * 128], BF16)
        self.wldb = sb("wldb", [128, 1024], BF16)
        self.wld = sb("wld", [128, 1024], F32)

    def _alloc_B(self):
        nc, es = self.nc, self.es
        sb = lambda n, s, d: es.enter_context(nc.sbuf_tensor(n, s, d))
        self.oT = sb("oT", [128, 2, DT, TILE], BF16)
        self.xnT = sb("xnT", [128, 2, DT, TILE], BF16)
        self.ybf = sb("ybf", [128, DT, TILE], BF16)
        self.hT = sb("hT", [128, DT, TILE], BF16)
        self.hgx = sb("hgx", [128, FT, TILE], BF16)
        self.gsb = sb("gsb", [128, DT, TILE], BF16)
        self.w1s = sb("w1s", [128, 2, DT, 512], BF16)
        self.w2s = sb("w2s", [128, 2, FT, 128], BF16)
        self.rw = sb("rw", [1, 4, TILE], F32)
        self.bca = sb("bca", [128, TILE], F32)
        self.bcc = sb("bcc", [128, TILE], F32)
        self.tt1 = sb("tt1", [128, TILE], BF16)
        self.tt2 = sb("tt2", [128, TILE], BF16)
        self.tt3 = sb("tt3", [128, TILE], BF16)
        self.xb2 = sb("xb2", [128, 2, D], F32)
        self.outb = sb("outb", [128, 2, D], F32)
        self.sgT = sb("sgT", [128, NCH, D], BF16)

    def load_act(self, sc, name):
        from concourse.hw_specs import get_activation_tables
        if not hasattr(self, "_act_names"):
            self._act_names = list(get_activation_tables(self.nc.m.arch).keys())
        idx = self._act_names.index(name)
        return sc.add_instruction(mybir.InstLoadActFuncSet(
            name=self.nc.get_next_instruction_name(), act_func_set_id=idx, ins=[], outs=[]))

    def vchain(self, eng, sem, ops):
        for op in ops:
            sem.inc(op())
            eng.wait_ge(sem.h, sem.n)

    def bw(self, eng, name):
        g = self.guard.get(name)
        if g is not None:
            eng.wait_ge(g[0].h, g[1])

    def bf_(self, name, sem):
        self.guard[name] = (sem, sem.n)

    def _phase0(self):
        s_ld, s_sp, s_tl, s_ac, s_dv, s_pe = (self.s_ld, self.s_sp, self.s_tl,
                                              self.s_ac, self.s_dv, self.s_pe)

        s_lw = self.s_lw

        @self.block.gpsimd
        def _(g):
            s_lw.inc(g.dma_start(out=self.wq[:, :, :], in_=self.d_wq.rearrange("(m p) c -> p m c", p=128)), 16)
            s_lw.inc(g.dma_start(out=self.wk[:, :, :], in_=self.d_wk.rearrange("(m p) c -> p m c", p=128)), 16)
            s_lw.inc(g.dma_start(out=self.wv[:, :, :], in_=self.d_wv.rearrange("(m p) c -> p m c", p=128)), 16)
            s_lw.inc(g.dma_start(out=self.wab[:, :, :], in_=self.d_wab.rearrange("(m p) c -> p m c", p=128)), 16)
            s_lw.inc(g.dma_start(out=self.ln1row[0:1, 0:D], in_=self.d_ln1w[0:1, :]), 16)
            s_lw.inc(g.dma_start(out=self.ln1row[0:1, D:2 * D], in_=self.d_ln1b[0:1, :]), 16)
            if self.do_b:
                s_lw.inc(g.dma_start(out=self.wo[:, :, :], in_=self.d_wo.rearrange("(m p) c -> p m c", p=128)), 16)
        self.ld_essential = s_lw.n

        @self.block.sync
        def _(sy):
            for src, dst in [
                (self.d_ident, self.ident), (self.d_identf, self.identf),
                (self.d_onescol, self.onescol), (self.d_onesrowb, self.onesrowb),
                (self.d_onesrowf, self.onesrowf), (self.d_ones4, self.ones4),
                (self.d_hmask, self.hmask), (self.d_mask_ls, self.mls),
                (self.d_mask_us, self.mus), (self.d_mask_ui, self.mui),
            ]:
                s_ld.inc(sy.dma_start(out=dst[:, :], in_=src[:, :]), 16)
            if self.do_b:
                for src, dst in [(self.d_b1c, self.b1c), (self.d_b2c, self.b2c),
                                 (self.d_ln2wc, self.ln2wc), (self.d_ln2bc, self.ln2bc)]:
                    s_ld.inc(sy.dma_start(out=dst[:, :], in_=src[:, :]), 16)
        self.ld_consts = s_ld.n

        @self.block.tensor
        def _(te):
            s_ld.wait(te, self.ld_consts)
            s_lw.wait(te, self.ld_essential)
            te.matmul(self.pb0[:, 0:512], self.onesrowb[0:1, :], self.ln1row[0:1, 0:512], start=True, stop=True)
            te.matmul(self.pb0[:, 512:1024], self.onesrowb[0:1, :], self.ln1row[0:1, 512:1024], start=True, stop=True)
            te.matmul(self.pb1[:, 0:512], self.onesrowb[0:1, :], self.ln1row[0:1, D:D + 512], start=True, stop=True)
            s_pe.inc(te.matmul(self.pb1[:, 512:1024], self.onesrowb[0:1, :], self.ln1row[0:1, D + 512:2 * D],
                               start=True, stop=True))

        @self.block.vector
        def _(v):
            v.memset(self.epscol[:, 0:1], EPS_LN)
            v.memset(self.epscol[:, 1:2], EPS_L2)
            v.memset(self.Sst[0][:, :, :], 0.0)
            v.memset(self.Sst[1][:, :, :], 0.0)
            s_pe.wait(v)
            v.tensor_copy(self.ln1w_bc[:, :], self.pb0[:, :])
            s_dv.inc(v.tensor_copy(self.ln1b_bc[:, :], self.pb1[:, :]))
        self.bf_("pb0", s_dv)
        self.bf_("pb1", s_dv)

        @self.block.scalar
        def _(sc):
            self.load_act(sc, "natural_log_exp_and_others")

    def _casts(self, part):
        if not self.do_b:
            return

        @self.block.gpsimd
        def _(g, part=part):
            if part == 0:
                for fb in range(8):
                    self.s_wc.inc(g.dma_start(
                        out=self.d_w1b[fb],
                        in_=self.d_w1[:, fb * 512:(fb + 1) * 512]
                        .rearrange("(m p) c -> p m c", p=128)), 16)
            else:
                for jd in range((part - 1) * 4, part * 4):
                    self.s_wc.inc(g.dma_start(
                        out=self.d_w2b[jd],
                        in_=self.d_w2[:, jd * 128:(jd + 1) * 128]
                        .rearrange("(i p) c -> p i c", p=128)), 16)

    def _sign_step(self, t):
        if not self.do_b or t > DT:
            return
        s_x, s_ac, s_sp = self.s_x, self.s_ac, self.s_sp

        @self.block.scalar
        def _(sc, t=t):
            if t >= 1:
                i = t - 1
                self.s_sn.wait(sc)
                self.bw(sc, "wldb")
                s_ac.inc(sc.activation(self.wldb[:, :], self.wld[:, :], AF.Sign))
                self.bf_("wld", s_ac)
                s_ac.wait(sc)
                self.s_sn.inc(sc.dma_start(out=self.d_swb[i * 128:(i + 1) * 128, :], in_=self.wldb[:, :]), 16)
                self.bf_("wldb", self.s_sn)
                if i == DT - 1:
                    self.bf_("swb_done", self.s_sn)
            if t < DT:
                self.bw(sc, "wld")
                self.s_sn.inc(sc.dma_start(out=self.wld[:, :], in_=self.d_ffnw[t * 128:(t + 1) * 128, :]), 16)

    def _front_pieces(self, t):
        def p0(t=t):
            self._ln_sub(t, 0)
            self._ln_sub(t, 1)

        def p1(t=t):
            self._ln_sub(t, 2)
            self._ln_sub(t, 3)
            self.bf_("xnb", self.s_sp)
            self.bf_("nn", self.s_sp)
            self._nt_load(t)

        def p2(t=t):
            self._proj(t)

        def p3(t=t):
            self._l2ab(t)
        return [p0, p1, p2, p3]

    def _ln_sub(self, t, s):
        s_x, s_sp, s_dv, s_ac = self.s_x, self.s_sp, self.s_dv, self.s_ac
        r0 = t * TILE
        par = s % 2

        @self.block.sync
        def _(sy, s=s, par=par):
            self.bw(sy, f"xs{par}")
            self.s_xp[par].inc(sy.dma_start(out=self.xs[:, par, :],
                                            in_=self.d_x[r0 + s * 128: r0 + (s + 1) * 128, :]), 16)

            @self.block.vector
            def _(v, s=s, par=par):
                self.s_xp[par].wait(v)
                o = par * 16
                st = lambda a, b: self.stat[:, o + a:o + b]
                x = self.xs[:, par, :]
                bn = self.bnb[:, par, :]
                v.bn_stats(bn[:, 0:6], x[:, 0:512])
                s_dv.inc(v.bn_stats(bn[:, 6:12], x[:, 512:1024]))
                v.wait_ge(s_dv.h, s_dv.n)
                s_dv.inc(v.bn_aggr(st(2, 4), bn.rearrange("p (g c) -> p g c", g=2)))

            @self.block.scalar
            def _(sc, s=s, par=par):
                s_dv.wait(sc)
                o = par * 16
                st = lambda a, b: self.stat[:, o + a:o + b]
                self.vchain(sc, s_ac, [
                    lambda: sc.activation(st(6, 7), st(3, 4), AF.Ln, bias=self.epscol[:, 0:1]),
                    lambda: sc.activation(st(7, 8), st(6, 7), AF.Exp, scale=-0.5),
                ])

            @self.block.vector
            def _(v, s=s, par=par):
                s_ac.wait(v)
                self.bw(v, "xnb")
                o = par * 16
                st = lambda a, b: self.stat[:, o + a:o + b]
                x = self.xs[:, par, :]
                xnb = self.xnb[:, s, :]
                bn = self.bnb[:, par, :]
                self.vchain(v, s_dv, [
                    lambda: v.tensor_scalar(xnb, x, st(2, 3), st(7, 8), OP.subtract, OP.mult),
                    lambda: v.bn_stats(bn[:, 0:6], xnb[:, 0:512]),
                ])
                s_dv.inc(v.bn_stats(bn[:, 6:12], xnb[:, 512:1024]))
                v.wait_ge(s_dv.h, s_dv.n)
                s_dv.inc(v.bn_aggr(st(10, 12), bn.rearrange("p (g c) -> p g c", g=2)))

            @self.block.scalar
            def _(sc, s=s, par=par):
                s_dv.wait(sc)
                o = par * 16
                st = lambda a, b: self.stat[:, o + a:o + b]
                self.vchain(sc, s_ac, [
                    lambda: sc.activation(st(14, 15), st(11, 12), AF.Ln, bias=self.epscol[:, 0:1]),
                    lambda: sc.activation(st(15, 16), st(14, 15), AF.Exp, scale=-0.5),
                ])

            @self.block.vector
            def _(v, s=s, par=par):
                s_ac.wait(v)
                self.bw(v, "nn")
                o = par * 16
                st = lambda a, b: self.stat[:, o + a:o + b]
                scr = self.xs[:, par, :]
                v.tensor_scalar(scr, self.xnb[:, s, :], st(10, 11),
                                st(15, 16), OP.subtract, OP.mult)
                v.tensor_tensor(self.nn[:, s, :], scr, self.ln1w_bc[:, :], OP.mult)
                s_dv.inc(v.tensor_tensor(self.nn[:, s, :], self.nn[:, s, :], self.ln1b_bc[:, :], OP.add))
                self.bf_(f"xs{par}", s_dv)

        @self.block.gpsimd
        def _(g, s=s):
            s_dv.wait(g)
            s_sp.inc(g.dma_start(out=self.d_xn[r0 + s * 128:r0 + (s + 1) * 128, :], in_=self.xnb[:, s, :]), 16)
            s_sp.inc(g.dma_start(out=self.d_n[r0 + s * 128:r0 + (s + 1) * 128, :], in_=self.nn[:, s, :]), 16)

    def _nt_load(self, t):
        s_sp, s_tl = self.s_sp, self.s_tl
        r0 = t * TILE

        @self.block.sync
        def _(sy):
            s_sp.wait(sy)
            self.bw(sy, "nT")
            s_tl.inc(sy.dma_start(out=self.nT[:, :, :], in_=self.d_n[r0:r0 + TILE, :], transpose=True), 16)

    def _proj(self, t):
        s_tl, s_pe, s_dv, s_ac = self.s_tl, self.s_pe, self.s_dv, self.s_ac
        tp = t % 2
        ifp, ifd = f"ifpe{tp}", f"ifdv{tp}"

        projdefs = [(self.qh, self.wq, 0), (self.qh, self.wq, 1),
                    (self.kh, self.wk, 0), (self.kh, self.wk, 1)]
        for idx, (dst, w, rnd) in enumerate(projdefs):
            pbank = self.pb0 if idx % 2 == 0 else self.pb1
            pname = "pb0" if idx % 2 == 0 else "pb1"

            @self.block.tensor
            def _(te, dst=dst, w=w, rnd=rnd, pbank=pbank, pname=pname):
                s_tl.wait(te)
                self.s_lw.wait(te, self.ld_essential)
                self.bw(te, pname)
                if pname == "pb0":
                    self.bw(te, "pb0h")
                    self.bw(te, "pb0c")
                ins = None
                for hh in range(2):
                    h = rnd * 2 + hh
                    for m in range(DT):
                        ins = te.matmul(pbank[:, hh * 512:(hh + 1) * 512],
                                        w[:, m, h * 128:(h + 1) * 128],
                                        self.nT[:, m, :],
                                        start=(m == 0), stop=(m == DT - 1))
                s_pe.inc(ins)

            @self.block.scalar
            def _(sc, idx=idx, dst=dst, rnd=rnd, pbank=pbank, pname=pname):
                s_pe.wait(sc)
                self.bw(sc, ifp)
                self.bw(sc, ifd)
                if idx == 0:
                    self.load_act(sc, "silu_and_others")
                dd = dst[:, tp, rnd * 2:(rnd + 1) * 2, :].rearrange("p a b -> p (a b)")
                s_ac.inc(sc.activation(dd, pbank[:, :], AF.Silu))
                self.bf_(pname, s_ac)

            @self.block.vector
            def _(v, idx=idx, dst=dst, rnd=rnd):
                s_ac.wait(v)
                self.bw(v, "sqs")
                dd = dst[:, tp, rnd * 2:(rnd + 1) * 2, :].rearrange("p a b -> p (a b)")
                s_dv.inc(v.tensor_tensor(self.sqs[:, idx * 1024:(idx + 1) * 1024], dd, dd, OP.mult))

        for s in range(NCH):
            pbank = self.p0 if s % 2 == 0 else self.p1
            pname = "p0" if s % 2 == 0 else "p1"

            @self.block.tensor
            def _(te, s=s, pbank=pbank, pname=pname):
                self.bw(te, pname)
                ins = None
                for m in range(DT):
                    ins = te.matmul(pbank[:, :], self.nT[:, m, s * 128:(s + 1) * 128],
                                    self.wv[:, m, :], start=(m == 0), stop=(m == DT - 1))
                s_pe.inc(ins)

            @self.block.scalar
            def _(sc, s=s, pbank=pbank, pname=pname):
                s_pe.wait(sc)
                self.bw(sc, ifp)
                self.bw(sc, ifd)
                s_ac.inc(sc.activation(self.vt[:, tp, s, :], pbank[:, :], AF.Silu))
                self.bf_(pname, s_ac)
                if s == NCH - 1:
                    self.load_act(sc, "natural_log_exp_and_others")

    def _l2ab(self, t):
        s_sp, s_tl, s_pe, s_dv, s_ac = self.s_sp, self.s_tl, self.s_pe, self.s_dv, self.s_ac
        tp = t % 2
        ifp, ifd = f"ifpe{tp}", f"ifdv{tp}"

        for idx, (dst, rnd) in enumerate([(self.qh, 0), (self.qh, 1), (self.kh, 0), (self.kh, 1)]):
            for hh in range(2):
                @self.block.tensor
                def _(te, idx=idx, hh=hh):
                    s_dv.wait(te)
                    self.bw(te, "p2")
                    s_pe.inc(te.matmul(self.p2[0:1, :], self.onescol[:, 0:1],
                                       self.sqs[:, idx * 1024 + hh * 512: idx * 1024 + (hh + 1) * 512],
                                       start=True, stop=True))

                @self.block.scalar
                def _(sc, idx=idx, hh=hh):
                    s_pe.wait(sc)
                    self.vchain(sc, s_ac, [
                        lambda: sc.activation(self.rowA[0:1, :], self.p2[0:1, :], AF.Ln,
                                              bias=self.epscol[0:1, 1:2]),
                        lambda: sc.activation(self.rowB[0:1, :], self.rowA[0:1, :], AF.Exp, scale=-0.5),
                    ])
                    self.bf_("p2", s_ac)

                @self.block.tensor
                def _(te, hh=hh):
                    s_ac.wait(te)
                    self.bw(te, "p3")
                    s_pe.inc(te.matmul(self.p3[:, :], self.onesrowb[0:1, :], self.rowB[0:1, :],
                                       start=True, stop=True))

                @self.block.vector
                def _(v, dst=dst, rnd=rnd, hh=hh):
                    s_pe.wait(v)
                    self.bw(v, ifp)
                    self.bw(v, ifd)
                    h = rnd * 2 + hh
                    s_dv.inc(v.tensor_tensor(dst[:, tp, h, :], dst[:, tp, h, :], self.p3[:, :], OP.mult))
                    self.bf_("p3", s_dv)

        @self.block.gpsimd
        def _(g):
            s_dv.wait(g)
            for h in range(HG):
                s_sp.inc(g.dma_start(out=self.d_kfm[t, h], in_=self.kh[:, tp, h, :]), 16)

        @self.block.sync
        def _(sy):
            s_sp.wait(sy)
            self.bw(sy, ifp)
            for h in range(HG):
                s_tl.inc(sy.dma_start(out=self.ktm[:, tp, h, :, :], in_=self.d_kfm[t, h], transpose=True), 16)

        @self.block.tensor
        def _(te):
            self.bw(te, "p2")
            for m in range(DT):
                te.matmul(self.p2[0:4, :], self.wab[:, m, 0:HG], self.nT[:, m, :],
                          start=(m == 0), stop=(m == DT - 1))
            self.bw(te, "p3")
            ins = None
            for m in range(DT):
                ins = te.matmul(self.p3[0:4, :], self.wab[:, m, HG:2 * HG], self.nT[:, m, :],
                                start=(m == 0), stop=(m == DT - 1))
            s_pe.inc(ins)

        @self.block.scalar
        def _(sc):
            s_pe.wait(sc)
            self.bw(sc, ifp)
            self.bw(sc, ifd)
            sc.activation(self.pa[0:4, tp, :], self.p2[0:4, :], AF.Exp, scale=-1.0)
            s_ac.inc(sc.activation(self.pbr[0:4, tp, :], self.p3[0:4, :], AF.Exp, scale=-1.0))
            self.bf_("p2", s_ac)
            self.bf_("p3", s_ac)

        @self.block.vector
        def _(v):
            s_ac.wait(v)
            v.tensor_scalar(self.cmb[0:4, tp, :], self.pa[0:4, tp, :], 1.0, None, OP.add)
            s_dv.inc(v.tensor_scalar(self.sfx[0:4, tp, :], self.pbr[0:4, tp, :], 1.0, None, OP.add))

        @self.block.scalar
        def _(sc):
            s_dv.wait(sc)
            sc.activation(self.pa[0:4, tp, :], self.cmb[0:4, tp, :], AF.Ln)
            s_ac.inc(sc.activation(self.pbr[0:4, tp, :], self.sfx[0:4, tp, :], AF.Ln))

        @self.block.vector
        def _(v):
            s_ac.wait(v)
            for g_ in range(NCH):
                sl = slice(g_ * 128, (g_ + 1) * 128)
                ins = v.tensor_tensor_scan(self.pcr[0:4, tp, sl], self.pa[0:4, tp, sl],
                                           self.pa[0:4, tp, sl], 0.0, OP.add, OP.bypass)
            s_dv.inc(ins)
            v.wait_ge(s_dv.h, s_dv.n)
            for g_ in range(NCH):
                sl = slice(g_ * 128, (g_ + 1) * 128)
                ins = v.tensor_scalar(self.sfx[0:4, tp, sl], self.pcr[0:4, tp, sl],
                                      self.pcr[0:4, tp, g_ * 128 + 127:g_ * 128 + 128], -1.0,
                                      OP.subtract, OP.mult)
            s_dv.inc(ins)
            v.wait_ge(s_dv.h, s_dv.n)
            s_dv.inc(v.scalar_tensor_tensor(self.cmb[0:4, tp, :], self.pcr[0:4, tp, :], -1.0,
                                            self.pbr[0:4, tp, :], OP.mult, OP.subtract))

        for g_ in range(NCH):
            @self.block.tensor
            def _(te, g_=g_):
                s_dv.wait(te)
                self.bw(te, "p3")
                sl = slice(g_ * 128, (g_ + 1) * 128)
                te.transpose(self.p3[:, 0:4], self.pcr[0:4, tp, sl], self.identf[0:4, 0:4])
                te.transpose(self.p3[:, 4:8], self.pbr[0:4, tp, sl], self.identf[0:4, 0:4])
                te.transpose(self.p3[:, 8:12], self.sfx[0:4, tp, sl], self.identf[0:4, 0:4])
                s_pe.inc(te.transpose(self.p3[:, 12:16], self.cmb[0:4, tp, sl], self.identf[0:4, 0:4]))

            @self.block.vector
            def _(v, g_=g_):
                s_pe.wait(v)
                s_dv.inc(v.tensor_copy(self.ccol[:, tp, g_, 0:16], self.p3[:, 0:16]))
                self.bf_("p3", s_dv)

            @self.block.vector
            def _(v, g_=g_):
                s_dv.wait(v)
                s_dv.inc(v.tensor_tensor(self.dcol[:, tp, g_, 24:28], self.ccol[:, tp, g_, 0:4],
                                         self.ccol[:, tp, g_, 8:12], OP.add))

            @self.block.scalar
            def _(sc, g_=g_):
                s_dv.wait(sc)
                sc.activation(self.dcol[:, tp, g_, 0:4], self.ccol[:, tp, g_, 4:8], AF.Exp, scale=-1.0)
                sc.activation(self.dcol[:, tp, g_, 4:8], self.ccol[:, tp, g_, 12:16], AF.Exp)
                sc.activation(self.dcol[:, tp, g_, 8:12], self.ccol[:, tp, g_, 0:4], AF.Exp, scale=-1.0)
                sc.activation(self.dcol[:, tp, g_, 12:16], self.ccol[:, tp, g_, 8:12], AF.Exp, scale=-1.0)
                s_ac.inc(sc.activation(self.dcol[:, tp, g_, 16:20], self.dcol[:, tp, g_, 24:28],
                                       AF.Exp, scale=-1.0))

        self.bf_("nT", s_pe)
        self.bf_("sqs", s_pe)
        self.bf_(f"acol{tp}", s_ac)
        self.fdv[t] = s_dv.n

    def _chunk_solve(self, t, g_):
        s_sp, s_tl, s_pe, s_dv, s_ac, s_pl = (self.s_sp, self.s_tl, self.s_pe,
                                              self.s_dv, self.s_ac, self.s_pl)
        sl = slice(g_ * 128, (g_ + 1) * 128)
        tp = t % 2
        tg = t * NCH + g_
        cp = tg % 2
        stages = []

        st = {}

        def stage_front():
            @self.block.vector
            def _(v):
                self.bw(v, "rm")
                v.tensor_tensor(self.rmp[0:4, :].rearrange("k (h j) -> k h j", h=HG),
                                self.pcr[0:4, tp, sl].unsqueeze(1).broadcast_to((4, HG, 128)),
                                self.hmask[0:4, :].rearrange("k (h j) -> k h j", h=HG), OP.mult)
                v.tensor_scalar(self.rmn[0:4, :], self.rmp[0:4, :], -1.0, None, OP.mult)
                s_dv.inc(v.tensor_tensor(self.rmc[0:4, :].rearrange("k (h j) -> k h j", h=HG),
                                         self.cmb[0:4, tp, sl].unsqueeze(1).broadcast_to((4, HG, 128)),
                                         self.hmask[0:4, :].rearrange("k (h j) -> k h j", h=HG), OP.mult))
            st["rm_dv"] = s_dv.n

            @self.block.tensor
            def _(te):
                te.wait_ge(s_dv.h, self.fdv[t])
                self.bw(te, "p0")
                for h in range(HG):
                    te.matmul(self.p0[:, h * 128:(h + 1) * 128], self.kh[:, tp, h, sl],
                              self.kh[:, tp, h, sl], start=True, stop=True)
                self.bw(te, "p1")
                ins = None
                for h in range(HG):
                    ins = te.matmul(self.p1[:, h * 128:(h + 1) * 128], self.kh[:, tp, h, sl],
                                    self.qh[:, tp, h, sl], start=True, stop=True)
                s_pe.inc(ins)

        def build_mat(rhs_rows, colap, pmat, mask, out, gguard=None):
            @self.block.tensor
            def _(te):
                te.wait_ge(s_dv.h, st["rm_dv"])
                self.bw(te, "p2")
                s_pe.inc(te.matmul(self.p2[:, :], self.ones4[0:4, :], rhs_rows, start=True, stop=True))
            pe0 = s_pe.n

            @self.block.vector
            def _(v):
                v.wait_ge(s_pe.h, pe0)
                ins = None
                for h in range(HG):
                    ins = v.tensor_scalar(self.argf[:, h * 128:(h + 1) * 128],
                                          self.p2[:, h * 128:(h + 1) * 128],
                                          colap[:, h:h + 1], 0.0, OP.add, OP.min)
                s_dv.inc(ins)
                self.bf_("p2", s_dv)
            dv1 = s_dv.n

            @self.block.scalar
            def _(sc):
                sc.wait_ge(s_dv.h, dv1)
                s_ac.inc(sc.activation(self.eM[:, :], self.argf[:, :], AF.Exp))
            ac1 = s_ac.n

            @self.block.vector
            def _(v):
                v.wait_ge(s_ac.h, ac1)
                self.bw(v, "tpb")
                s_dv.inc(v.tensor_tensor(self.tp[:, :], self.eM[:, :], pmat[:, :], OP.mult))
            dv2 = s_dv.n

            @self.block.gpsimd
            def _(g):
                g.wait_ge(s_dv.h, dv2)
                if gguard is not None:
                    self.bw(g, gguard)
                s_pl.inc(g.tensor_tensor(out[:, :].rearrange("p (h j) -> p h j", h=HG),
                                         self.tp[:, :].rearrange("p (h j) -> p h j", h=HG),
                                         mask[:, :].unsqueeze(1).broadcast_to((128, HG, 128)),
                                         OP.mult))
            self.bf_("tpb", s_pl)

        def stage_bm0():
            build_mat(self.rmp[0:4, :], self.ccol[:, tp, g_, 12:16], self.p0, self.mls, self.Mx,
                      gguard="MxF")

        def stage_bm1():
            build_mat(self.rmc[0:4, :], self.ccol[:, tp, g_, 0:4], self.p0, self.mus, self.MTx)
            build_mat(self.rmn[0:4, :], self.ccol[:, tp, g_, 0:4], self.p1, self.mui,
                      self.NTx[:, cp], gguard=f"NTx{cp}")
            self.bf_("rm", s_pl)
            self.bf_("p0", s_dv)
            self.bf_("p1", s_dv)
            st["bm_pl"] = s_pl.n

        PwA, PwB = self.Pw[0], self.Pw[1]
        PwT = self.Pw[2:7]
        PwF = self.Mx
        chains = [(self.Mx, self.MTx, PwA, PwT[0]), (PwA, PwT[0], PwB, PwT[1]),
                  (PwB, PwT[1], PwA, PwT[2]), (PwA, PwT[2], PwB, PwT[3]),
                  (PwB, PwT[3], PwA, PwT[4])]

        def mk_sq(k, P_, PT, oP, oPT):
            def stage():
                @self.block.tensor
                def _(te):
                    if k == 0:
                        te.wait_ge(s_pl.h, st["bm_pl"])
                    else:
                        te.wait_ge(s_dv.h, st[f"sq{k - 1}_dv"])
                        te.wait_ge(s_ac.h, st[f"sq{k - 1}_ac"])
                    self.bw(te, "p3")
                    for h in range(HG):
                        hs = slice(h * 128, (h + 1) * 128)
                        te.matmul(self.p3[:, hs], PT[:, hs], P_[:, hs], start=True, stop=True)
                    self.bw(te, "pb0h")
                    self.bw(te, "pb0")
                    ins = None
                    for h in range(HG):
                        hs = slice(h * 128, (h + 1) * 128)
                        ins = te.matmul(self.pb0[:, 512:1024][:, hs], P_[:, hs], PT[:, hs],
                                        start=True, stop=True)
                    s_pe.inc(ins)
                pe1 = s_pe.n

                @self.block.vector
                def _(v):
                    v.wait_ge(s_pe.h, pe1)
                    s_dv.inc(v.tensor_copy(oP[:, :], self.p3[:, :]))
                    self.bf_("p3", s_dv)
                st[f"sq{k}_dv"] = s_dv.n

                @self.block.scalar
                def _(sc):
                    sc.wait_ge(s_pe.h, pe1)
                    s_ac.inc(sc.copy(oPT[:, :], self.pb0[:, 512:1024]))
                    self.bf_("pb0h", s_ac)
                st[f"sq{k}_ac"] = s_ac.n
            return stage

        def stage_r0():
            @self.block.vector
            def _(v):
                s_tl.wait(v)
                self.bw(v, f"acol{tp}")
                self.bw(v, "Rab")
                ins = None
                for h in range(HG):
                    v.tensor_scalar(self.Ra[:, h, 0:128], self.vt[:, tp, g_, h * 128:(h + 1) * 128],
                                    self.dcol[:, tp, g_, 0 + h:1 + h], None, OP.mult)
                    ins = v.tensor_scalar(self.Ra[:, h, 128:256], self.ktm[:, tp, h, g_, :],
                                          self.dcol[:, tp, g_, 4 + h:5 + h], None, OP.mult)
                s_dv.inc(ins)
            st["ap-1_dep"] = (s_dv, s_dv.n)

        def stage_fin():
            @self.block.tensor
            def _(te):
                te.wait_ge(s_dv.h, st["sq4_dv"])
                te.wait_ge(s_ac.h, st["sq4_ac"])
                self.bw(te, "p3")
                ins = None
                for h in range(HG):
                    hs = slice(h * 128, (h + 1) * 128)
                    ins = te.matmul(self.p3[:, hs], PwA[:, hs], PwT[4][:, hs], start=True, stop=True)
                s_pe.inc(ins)
            pef = s_pe.n

            @self.block.vector
            def _(v):
                v.wait_ge(s_pe.h, pef)
                s_dv.inc(v.tensor_copy(PwF[:, :], self.p3[:, :]))
                self.bf_("p3", s_dv)
            st["fin_dv"] = s_dv.n

        seq = [(self.MTx, -1.0), (PwT[0], 1.0), (PwT[1], 1.0), (PwT[2], 1.0),
               (PwT[3], 1.0), (PwT[4], 1.0), (PwF, 1.0)]

        def mk_apply(i, lhsT, sgn, cur, nxt):
            def stage():
                @self.block.tensor
                def _(te):
                    dep = st[f"ap{i - 1}_dep"]
                    te.wait_ge(dep[0].h, dep[1])
                    if i == 0:
                        te.wait_ge(s_pl.h, st["bm_pl"])
                    elif i <= 5:
                        te.wait_ge(s_ac.h, st[f"sq{i - 1}_ac"])
                    else:
                        te.wait_ge(s_dv.h, st["fin_dv"])
                    self.bw(te, "pb1")
                    ins = None
                    for h in range(HG):
                        hs = slice(h * 128, (h + 1) * 128)
                        ins = te.matmul(self.pb1[:, h * 256:(h + 1) * 256], lhsT[:, hs], cur[:, h, :],
                                        start=True, stop=True)
                    s_pe.inc(ins)
                pea = s_pe.n

                @self.block.vector
                def _(v):
                    v.wait_ge(s_pe.h, pea)
                    s_dv.inc(v.scalar_tensor_tensor(nxt[:, :, :],
                                                    self.pb1[:, :].rearrange("p (h j) -> p h j", h=HG),
                                                    sgn, cur[:, :, :],
                                                    OP.mult, OP.add))
                    self.bf_("pb1", s_dv)
                st[f"ap{i}_dep"] = (s_dv, s_dv.n)
            return stage

        apply_stages = []
        cur, nxt = self.Ra, self.Rb
        for i, (lhsT, sgn) in enumerate(seq):
            apply_stages.append(mk_apply(i, lhsT, sgn, cur, nxt))
            cur, nxt = nxt, cur
        Rfin = cur

        def stage_wmt():
            @self.block.tensor
            def _(te):
                dep = st["ap6_dep"]
                te.wait_ge(dep[0].h, dep[1])
                self.bw(te, "p2")
                ins = None
                for h in range(HG):
                    ins = te.matmul(self.p2[:, h * 128:(h + 1) * 128], Rfin[:, h, 128:256],
                                    self.ident[:, :], start=True, stop=True)
                s_pe.inc(ins)
            pew = s_pe.n

            @self.block.vector
            def _(v):
                v.wait_ge(s_pe.h, pew)
                self.bw(v, f"WmT{cp}")
                self.bw(v, f"Uf{cp}")
                v.tensor_copy(self.WmT[:, cp].rearrange("p h c -> p (h c)"), self.p2[:, :])
                s_dv.inc(v.tensor_copy(self.Ufree[:, cp], Rfin[:, :, 0:128]))
                self.bf_("p2", s_dv)
                self.bf_("Rab", s_dv)
            self.bf_("MxF", s_pe)
            self.sdv_solve[tg] = s_dv.n

        sq = [mk_sq(k, *c) for k, c in enumerate(chains)]
        ap = apply_stages
        stages = [stage_front, stage_bm0, stage_bm1, stage_r0,
                  sq[0], ap[0], sq[1], ap[1], sq[2], ap[2], sq[3], ap[3], sq[4], ap[4],
                  stage_fin, ap[5], ap[6], stage_wmt]
        return stages

    def _chunk_chain(self, t, g_):
        s_sp, s_pe, s_dv = self.s_sp, self.s_pe, self.s_dv
        sl = slice(g_ * 128, (g_ + 1) * 128)
        tp = t % 2
        tg = t * NCH + g_
        cp = tg % 2
        Scur = self.Sst[tg % 2]
        Snew = self.Sst[(tg + 1) % 2]
        st = {}

        def c0():
            @self.block.tensor
            def _(te):
                te.wait_ge(s_dv.h, max(self.sdv_solve[tg], self.sdv_snew.get(tg - 1, 0)))
                self.bw(te, "pb0")
                self.bw(te, "pb0c")
                ins = None
                for h in range(HG):
                    ins = te.matmul(self.pb0[:, h * 128:(h + 1) * 128], self.WmT[:, cp, h, :],
                                    Scur[:, h, :], start=True, stop=True)
                s_pe.inc(ins)
            self.bf_(f"WmT{cp}", s_pe)
            st["c0"] = s_pe.n

        def c1():
            @self.block.vector
            def _(v):
                v.wait_ge(s_pe.h, st["c0"])
                v.scalar_tensor_tensor(self.Uu[:, :, :],
                                       self.pb0[:, 0:512].rearrange("p (h j) -> p h j", h=HG),
                                       -1.0, self.Ufree[:, cp],
                                       OP.mult, OP.add)
                ins = None
                for h in range(HG):
                    ins = v.tensor_scalar(self.Ug[:, h, :], self.Uu[:, h, :],
                                          self.dcol[:, tp, g_, 12 + h:13 + h], None, OP.mult)
                s_dv.inc(ins)
                self.bf_("pb0c", s_dv)
                self.bf_(f"Uf{cp}", s_dv)
            st["c1"] = s_dv.n

        def c2():
            @self.block.tensor
            def _(te):
                te.wait_ge(s_dv.h, st["c1"])
                self.bw(te, "p3")
                ins = None
                for h in range(HG):
                    ins = te.matmul(self.p3[:, h * 128:(h + 1) * 128], self.ktm[:, tp, h, g_, :],
                                    self.Ug[:, h, :], start=True, stop=True)
                s_pe.inc(ins)
            st["c2"] = s_pe.n

        def c3():
            @self.block.vector
            def _(v):
                v.wait_ge(s_pe.h, st["c2"])
                ins = None
                for h in range(HG):
                    ins = v.scalar_tensor_tensor(Snew[:, h, :], Scur[:, h, :],
                                                 self.dcol[:, tp, g_, 16 + h:17 + h],
                                                 self.p3[:, h * 128:(h + 1) * 128], OP.mult, OP.add)
                s_dv.inc(ins)
                self.bf_("p3", s_dv)
            self.sdv_snew[tg] = s_dv.n

        def c4():
            @self.block.tensor
            def _(te):
                te.wait_ge(s_dv.h, self.sdv_snew[tg])
                self.bw(te, "p0")
                for h in range(HG):
                    te.matmul(self.p0[:, h * 128:(h + 1) * 128],
                              self.NTx[:, cp, h * 128:(h + 1) * 128],
                              self.Uu[:, h, :], start=True, stop=True)
                self.bw(te, "p1")
                ins = None
                for h in range(HG):
                    ins = te.matmul(self.p1[:, h * 128:(h + 1) * 128], self.qh[:, tp, h, sl],
                                    Scur[:, h, :], start=True, stop=True)
                s_pe.inc(ins)
            self.bf_(f"NTx{cp}", s_pe)
            st["c4"] = s_pe.n

            @self.block.vector
            def _(v):
                v.wait_ge(s_pe.h, st["c4"])
                self.bw(v, "Ox")
                for h in range(HG):
                    v.tensor_scalar(self.t_o[:, h * 128:(h + 1) * 128],
                                    self.p1[:, h * 128:(h + 1) * 128],
                                    self.dcol[:, tp, g_, 8 + h:9 + h], None, OP.mult)
                s_dv.inc(v.tensor_tensor(self.Ox[:, :], self.t_o[:, :], self.p0[:, :], OP.add))
                self.bf_("p0", s_dv)
                self.bf_("p1", s_dv)

            @self.block.gpsimd
            def _(g):
                s_dv.wait(g)
                r = t * TILE + g_ * 128
                s_sp.inc(g.dma_start(out=self.d_rec[r:r + 128, :], in_=self.Ox[:, :]), 16)
            self.bf_("Ox", s_sp)
        return [c0, c1, c2, c3, c4]

    def _exchange_half(self):
        s_sp, s_cc = self.s_sp, self.s_cc
        half = self.nta * TILE // 2
        spv = s_sp.n

        @self.block.gpsimd
        def _(g):
            g.wait_ge(s_sp.h, spv)
            ins = g.collective_compute(
                "AllGather", mybir.AluOpType.bypass,
                replica_groups=[[0, 1], [2, 3], [4, 5], [6, 7]],
                ins=[self.d_rec[0:half, :].opt()],
                outs=[self.d_ag[0:2 * half, :].opt()],
            )
            s_cc.inc(ins, 1)

    def _exchange(self):
        s_sp, s_cc = self.s_sp, self.s_cc
        half = self.nta * TILE // 2

        @self.block.gpsimd
        def _(g):
            s_sp.wait(g)
            ins = g.collective_compute(
                "AllGather", mybir.AluOpType.bypass,
                replica_groups=[[0, 1], [2, 3], [4, 5], [6, 7]],
                ins=[self.d_rec[half:2 * half, :].opt()],
                outs=[self.d_ag[2 * half:4 * half, :].opt()],
            )
            s_cc.inc(ins, 1)
            g.wait_ge(s_cc.h, s_cc.n)
            reg = g.alloc_register()
            reg2 = g.alloc_register()
            g.load(reg, self.d_poff[0:1, 0:1])
            g.load(reg2, self.d_poff2[0:1, 0:1])
            v1 = g.snap(reg)
            v2 = g.snap(reg2)
            nb = self.ntb * TILE
            s_cc.inc(g.dma_start(out=self.d_oown[:, 0:HG * KD],
                                 in_=self.d_ag[bass.ds(v1, nb), :]), 16)
            s_cc.inc(g.dma_start(out=self.d_oown[:, HG * KD:D],
                                 in_=self.d_ag[bass.ds(v1 + half, nb), :]), 16)
            s_cc.inc(g.dma_start(out=self.d_xnown[:, :],
                                 in_=self.d_xn[bass.ds(v2, nb), :]), 16)
            g.wait_ge(s_cc.h, s_cc.n)

        @self.block.sync
        def _(sy):
            self.bw(sy, "swb_done")
            for i in range(DT):
                self.s_tl.inc(sy.dma_start(out=self.swt[:, :, i * 128:(i + 1) * 128],
                                           in_=self.d_swb[i * 128:(i + 1) * 128, :], transpose=True), 16)
        self.tl_swt = self.s_tl.n

    def _loadB(self, t):
        s_cc = self.s_cc
        slot = t % 2
        sem = self.s_ta[slot]
        r0 = t * TILE

        @self.block.scalar
        def _(sc, t=t, slot=slot, r0=r0, sem=sem):
            if t == 0:
                s_cc.wait(sc)
            self.bw(sc, f"oTs{slot}")
            sem.inc(sc.dma_start(out=self.oT[:, slot], in_=self.d_oown[r0:r0 + TILE, :],
                                 transpose=True), 16)
            self.bw(sc, f"xnP{slot}")
            self.bw(sc, f"xnV{slot}")
            sem.inc(sc.dma_start(out=self.xnT[:, slot], in_=self.d_xnown[r0:r0 + TILE, :],
                                 transpose=True), 16)
        self.tlB[t] = sem.n

    def _w1load(self, t, fb):
        wslot = fb % 2
        sem = self.s_w1[wslot]

        @self.block.sync
        def _(sy, fb=fb, wslot=wslot, sem=sem):
            sy.wait_ge(self.s_wc.h, 256)
            self.bw(sy, f"w1s{wslot}")
            sem.inc(sy.dma_start(out=self.w1s[:, wslot], in_=self.d_w1b[fb]), 16)
        self.wbv[(t, "w1", fb)] = sem.n

    def _w2load(self, t, jd):
        wslot = jd % 2
        sem = self.s_w2[wslot]

        @self.block.sync
        def _(sy, jd=jd, wslot=wslot, sem=sem):
            sy.wait_ge(self.s_wc.h, 256)
            self.bw(sy, f"w2s{wslot}")
            sem.inc(sy.dma_start(out=self.w2s[:, wslot], in_=self.d_w2b[jd]), 16)
        self.wbv[(t, "w2", jd)] = sem.n

    def _tileB(self, t):
        s_x, s_tl, s_pe, s_dv, s_ac, s_pl, s_wb = (self.s_x, self.s_tl, self.s_pe, self.s_dv,
                                                   self.s_ac, self.s_pl, self.s_wb)
        r0 = t * TILE
        slot = t % 2
        if t + 1 < self.ntb:
            self._loadB(t + 1)
        self._w1load(t, 0)
        self._w1load(t, 1)

        for jd in range(DT):
            pbank = self.p0 if jd % 2 == 0 else self.p1
            pname = "p0" if jd % 2 == 0 else "p1"

            @self.block.tensor
            def _(te, jd=jd, pbank=pbank, pname=pname):
                te.wait_ge(self.s_ta[slot].h, self.tlB[t])
                self.s_lw.wait(te, self.ld_essential)
                self.bw(te, pname)
                ins = None
                for m in range(DT):
                    ins = te.matmul(pbank[:, :], self.wo[:, m, jd * 128:(jd + 1) * 128],
                                    self.oT[:, slot, m, :], start=(m == 0), stop=(m == DT - 1))
                s_pe.inc(ins)

            @self.block.vector
            def _(v, jd=jd, pbank=pbank, pname=pname):
                s_pe.wait(v)
                self.bw(v, "ybfT")
                s_dv.inc(v.tensor_tensor(self.ybf[:, jd, :], pbank[:, :],
                                         self.xnT[:, slot, jd, :], OP.add))
                self.bf_(pname, s_dv)
        self.bf_(f"oTs{slot}", s_pe)
        self.bf_(f"xnV{slot}", s_dv)

        @self.block.vector
        def _(v):
            s_dv.wait(v)
            self.bw(v, "hgx")
            s_dv.inc(v.tensor_tensor(self.hgx[:, 0:DT, :].rearrange("p m c -> p (m c)"),
                                     self.ybf[:, :, :].rearrange("p m c -> p (m c)"),
                                     self.ybf[:, :, :].rearrange("p m c -> p (m c)"), OP.mult))

        @self.block.tensor
        def _(te):
            s_dv.wait(te)
            self.bw(te, "p2")
            for m in range(DT):
                te.matmul(self.p2[0:1, :], self.onescol[:, 0:1], self.ybf[:, m, :],
                          start=(m == 0), stop=(m == DT - 1))
            self.bw(te, "p3")
            ins = None
            for m in range(DT):
                ins = te.matmul(self.p3[0:1, :], self.onescol[:, 0:1], self.hgx[:, m, :],
                                start=(m == 0), stop=(m == DT - 1))
            s_pe.inc(ins)
        self.bf_("ybfT", s_pe)
        stat_pe = s_pe.n

        for jo in range(DT):
            half = jo % 2

            @self.block.tensor
            def _(te, jo=jo, half=half):
                if jo == 0:
                    self.bw(te, "pb0")
                    self.bw(te, "pb0c")
                    te.wait_ge(s_tl.h, self.tl_swt)
                self.bw(te, f"gb{half}")
                ins = None
                for m in range(DT):
                    ins = te.matmul(self.pb0[:, half * 512:(half + 1) * 512],
                                    self.swt[:, m, jo * 128:(jo + 1) * 128],
                                    self.xnT[:, slot, m, :], start=(m == 0), stop=(m == DT - 1))
                s_pe.inc(ins)

            @self.block.scalar
            def _(sc, jo=jo, half=half):
                s_pe.wait(sc)
                self.bw(sc, "tt1")
                if jo == 0 and t == 0:
                    self.load_act(sc, "gelu_and_others")
                s_ac.inc(sc.activation(self.tt1[:, :], self.pb0[:, half * 512:(half + 1) * 512],
                                       AF.Tanh, scale=0.5))

            @self.block.vector
            def _(v, jo=jo, half=half):
                s_ac.wait(v)
                self.bw(v, "gsb")
                self.bw(v, "tt2g")
                v.tensor_scalar(self.tt2[:, :], self.tt1[:, :], 0.5, 0.5, OP.mult, OP.add)
                s_dv.inc(v.tensor_tensor(self.gsb[:, jo, :], self.tt2[:, :],
                                         self.pb0[:, half * 512:(half + 1) * 512], OP.mult))
                self.bf_(f"gb{half}", s_dv)
                self.bf_("tt1", s_dv)
        self.bf_(f"xnP{slot}", s_pe)

        @self.block.vector
        def _(v):
            v.wait_ge(s_pe.h, stat_pe)
            self.vchain(v, s_dv, [
                lambda: v.tensor_scalar(self.rw[0:1, 0, :], self.p2[0:1, :], 1.0 / D, None, OP.mult),
                lambda: v.tensor_scalar(self.rw[0:1, 1, :], self.p3[0:1, :], 1.0 / D, None, OP.mult),
                lambda: v.tensor_tensor(self.rw[0:1, 2, :], self.rw[0:1, 0, :], self.rw[0:1, 0, :], OP.mult),
                lambda: v.tensor_tensor(self.rw[0:1, 1, :], self.rw[0:1, 1, :], self.rw[0:1, 2, :], OP.subtract),
            ])
            self.bf_("p2", s_dv)
            self.bf_("p3", s_dv)

        @self.block.scalar
        def _(sc):
            s_dv.wait(sc)
            self.load_act(sc, "natural_log_exp_and_others")
            self.vchain(sc, s_ac, [
                lambda: sc.activation(self.rw[0:1, 2, :], self.rw[0:1, 1, :], AF.Ln,
                                      bias=self.epscol[0:1, 0:1]),
                lambda: sc.activation(self.rw[0:1, 1, :], self.rw[0:1, 2, :], AF.Exp, scale=-0.5),
            ])

        @self.block.vector
        def _(v):
            s_ac.wait(v)
            s_dv.inc(v.scalar_tensor_tensor(self.rw[0:1, 2, :], self.rw[0:1, 0, :], -1.0,
                                            self.rw[0:1, 1, :], OP.mult, OP.mult))

        @self.block.tensor
        def _(te):
            s_dv.wait(te)
            self.bw(te, "bc")
            if t == 0:
                self.bw(te, "pb1")
            te.matmul(self.pb1[:, 0:512], self.onesrowf[0:1, :], self.rw[0:1, 1, :],
                      start=True, stop=True)
            s_pe.inc(te.matmul(self.pb1[:, 512:1024], self.onesrowf[0:1, :], self.rw[0:1, 2, :],
                               start=True, stop=True))

        @self.block.vector
        def _(v):
            s_pe.wait(v)
            self.bw(v, "bcc")
            v.tensor_copy(self.bca[:, :], self.pb1[:, 0:512])
            s_dv.inc(v.tensor_copy(self.bcc[:, :], self.pb1[:, 512:1024]))
            self.bf_("bc", s_dv)

        for jd in range(DT):
            @self.block.vector
            def _(v, jd=jd):
                s_dv.wait(v)
                self.bw(v, "hT")
                s_dv.inc(v.tensor_tensor(self.tt3[:, :], self.ybf[:, jd, :], self.bca[:, :], OP.mult))

            @self.block.gpsimd
            def _(g, jd=jd):
                s_dv.wait(g)
                s_pl.inc(g.tensor_tensor(self.tt2[:, :], self.tt3[:, :], self.bcc[:, :], OP.add))

            @self.block.vector
            def _(v, jd=jd):
                s_pl.wait(v)
                s_dv.inc(v.tensor_scalar(self.hT[:, jd, :], self.tt2[:, :],
                                         self.ln2wc[:, jd:jd + 1], self.ln2bc[:, jd:jd + 1],
                                         OP.mult, OP.add))
        self.bf_("bcc", s_pl)

        for fb in range(8):
            fslot = fb % 2
            for i2 in range(4):
                ft = fb * 4 + i2
                pbank = self.p0 if ft % 2 == 0 else self.p1
                pname = "p0" if ft % 2 == 0 else "p1"

                @self.block.tensor
                def _(te, fb=fb, fslot=fslot, i2=i2, pbank=pbank, pname=pname):
                    te.wait_ge(self.s_w1[fslot].h, self.wbv[(t, "w1", fb)])
                    s_dv.wait(te)
                    self.bw(te, pname)
                    ins = None
                    for m in range(DT):
                        ins = te.matmul(pbank[:, :], self.w1s[:, fslot, m, i2 * 128:(i2 + 1) * 128],
                                        self.hT[:, m, :], start=(m == 0), stop=(m == DT - 1))
                    s_pe.inc(ins)

                @self.block.scalar
                def _(sc, ft=ft, pbank=pbank, pname=pname):
                    s_pe.wait(sc)
                    if ft == 0:
                        self.load_act(sc, "gelu_and_others")
                    s_ac.inc(sc.activation(self.hgx[:, ft, :], pbank[:, :], AF.Gelu,
                                           bias=self.b1c[:, ft:ft + 1]))
                    self.bf_(pname, s_ac)
            self.bf_(f"w1s{fslot}", self.s_pe)
            if fb + 2 < 8:
                self._w1load(t, fb + 2)
            if fb == 1:
                self._w2load(t, 0)
                self._w2load(t, 1)
        self.bf_("hT", self.s_pe)

        for jd in range(DT):
            pbank = self.p2 if jd % 2 == 0 else self.p3
            pname = "p2" if jd % 2 == 0 else "p3"
            wslot = jd % 2

            @self.block.tensor
            def _(te, jd=jd, pbank=pbank, pname=pname, wslot=wslot):
                te.wait_ge(self.s_w2[wslot].h, self.wbv[(t, "w2", jd)])
                s_ac.wait(te)
                self.bw(te, pname)
                ins = None
                for i in range(FT):
                    ins = te.matmul(pbank[:, :], self.w2s[:, wslot, i, :], self.hgx[:, i, :],
                                    start=(i == 0), stop=(i == FT - 1))
                s_pe.inc(ins)
            self.bf_(f"w2s{wslot}", self.s_pe)
            if jd + 2 < DT:
                self._w2load(t, jd + 2)

            @self.block.vector
            def _(v, jd=jd, pbank=pbank, pname=pname):
                s_pe.wait(v)
                v.scalar_tensor_tensor(self.tt3[:, :], pbank[:, :], 1.0, self.ybf[:, jd, :],
                                       OP.mult, OP.add)
                s_dv.inc(v.tensor_scalar(self.tt2[:, :], self.tt3[:, :],
                                         self.b2c[:, jd:jd + 1], None, OP.add))
                self.bf_(pname, s_dv)

            @self.block.gpsimd
            def _(g, jd=jd):
                s_dv.wait(g)
                s_pl.inc(g.tensor_tensor(self.gsb[:, jd, :], self.gsb[:, jd, :],
                                         self.tt2[:, :], OP.mult))
        self.bf_("hgx", self.s_pe)
        self.bf_("gsb", self.s_pl)
        self.bf_("tt2g", self.s_pl)

        @self.block.gpsimd
        def _(g):
            s_pl.wait(g)
            self.s_sp.inc(g.dma_start(out=self.d_sg[t].rearrange("(m p) c -> p m c", p=128),
                                      in_=self.gsb[:, :, :]), 16)

        @self.block.sync
        def _(sy):
            self.s_sp.wait(sy)
            self.bw(sy, "sgT")
            s_tl.inc(sy.dma_start(out=self.sgT[:, :, :], in_=self.d_sg[t], transpose=True), 16)
        tl_sg = s_tl.n

        for s in range(NCH):
            par = s % 2

            @self.block.sync
            def _(sy, s=s, par=par):
                self.bw(sy, f"xb{par}")
                self.s_xp[par].inc(sy.dma_start(out=self.xb2[:, par, :],
                                                in_=self.d_xown[r0 + s * 128:r0 + (s + 1) * 128, :]), 16)

            @self.block.vector
            def _(v, s=s, par=par):
                self.s_xp[par].wait(v)
                v.wait_ge(s_tl.h, tl_sg)
                self.bw(v, f"ob{par}")
                s_dv.inc(v.tensor_tensor(self.outb[:, par, :], self.xb2[:, par, :],
                                         self.sgT[:, s, :], OP.add))
                self.bf_(f"xb{par}", s_dv)

            @self.block.sync
            def _(sy, s=s, par=par):
                s_dv.wait(sy)
                self.s_st.inc(sy.dma_start(out=self.d_out[r0 + s * 128:r0 + (s + 1) * 128, :],
                                           in_=self.outb[:, par, :]), 16)
                self.bf_(f"ob{par}", self.s_st)
        self.bf_("sgT", s_dv)

    def _tapA(self):
        if not self.taps:
            return

        @self.block.sync
        def _(sy):
            self.s_dv.wait(sy)
            self.s_ac.wait(sy)
            self.s_pe.wait(sy)
            self.s_pl.wait(sy)
            self.s_sp.wait(sy)
            for (name, src, shape, dt) in self.taps:
                ap = src(self) if callable(src) else getattr(self, src)
                self.s_st.inc(sy.dma_start(out=self.d_taps[name][:, :] if len(shape) == 2
                                           else self.d_taps[name][:], in_=ap), 16)

    def _finish(self):
        if self.dbg_dram:
            nc = self.nc
            dbgs = [("dbg_rec", self.d_rec), ("dbg_ag", self.d_ag),
                    ("dbg_oown", self.d_oown), ("dbg_xnown", self.d_xnown),
                    ("dbg_xn", self.d_xn)]
            exts = {}
            for nm, srcap in dbgs:
                exts[nm] = nc.dram_tensor(nm, list(srcap.shape), srcap.dtype, kind="ExternalOutput").ap()

            @self.block.gpsimd
            def _(g):
                self.s_cc.wait(g)
                self.s_sp.wait(g)
                self.s_dv.wait(g)
                for nm, srcap in dbgs:
                    self.s_st.inc(g.dma_start(out=exts[nm][:, :], in_=srcap[:, :]), 16)

        if self.tapsB:
            @self.block.sync
            def _(sy):
                self.s_dv.wait(sy)
                self.s_ac.wait(sy)
                self.s_pe.wait(sy)
                self.s_pl.wait(sy)
                for (name, src, shape, dt) in self.tapsB:
                    ap = src(self) if callable(src) else getattr(self, src)
                    self.s_st.inc(sy.dma_start(out=self.d_taps[name][:, :] if len(shape) == 2
                                               else self.d_taps[name][:], in_=ap), 16)

        @self.block.sync
        def _(sy):
            self.s_st.wait(sy)
            self.s_sp.wait(sy)
            self.s_ld.wait(sy)
            self.s_lw.wait(sy)


def _consts():
    ii = np.arange(128)
    c = {
        "ident_bf": np.eye(128, dtype=bfd),
        "ident_f32": np.eye(128, dtype=np.float32),
        "ones_col": np.ones((128, 1), dtype=bfd),
        "ones_row_bf": np.ones((1, 128), dtype=bfd),
        "ones_row_f32": np.ones((1, 128), dtype=np.float32),
        "ones4_f32": np.ones((4, 128), dtype=np.float32),
        "mask_ls": (ii[None, :] < ii[:, None]).astype(bfd),
        "mask_us": (ii[None, :] > ii[:, None]).astype(bfd),
        "mask_ui": (ii[None, :] >= ii[:, None]).astype(bfd),
    }
    hm = np.zeros((4, HG * 128), dtype=np.float32)
    for k in range(4):
        hm[k, k * 128:(k + 1) * 128] = 1.0
    c["hmask_f32"] = hm
    return c


def _run(x, Wq, Wk, Wv, Wa, Wb, Wo, ln1w, ln1b, ln2w, ln2b, W1, b1, W2, b2, ffnw):
    Tn = x.shape[1]
    nta = Tn // TILE
    half = Tn // 2
    consts = _consts()
    bld = Builder(nta=nta)
    nc = bld.build()
    in_maps = []
    for c in range(8):
        b, q = c // 2, c % 2
        hs = slice(q * HG * KD, (q + 1) * HG * KD)
        m = {
            "x_full": x[b],
            "x_own": np.ascontiguousarray(x[b, q * half:(q + 1) * half]),
            "wq": np.ascontiguousarray(Wq[:, hs]),
            "wk": np.ascontiguousarray(Wk[:, hs]),
            "wv": np.ascontiguousarray(Wv[:, hs]),
            "wab": np.ascontiguousarray(np.concatenate(
                [Wa[:, q * HG:(q + 1) * HG], Wb[:, q * HG:(q + 1) * HG]], axis=1)),
            "wo": Wo,
            "w1": W1,
            "w2": W2,
            "b1c": np.ascontiguousarray(b1.reshape(FT, 128).T),
            "b2c": np.ascontiguousarray(b2.reshape(DT, 128).T),
            "ln1w": ln1w[None, :],
            "ln1b": ln1b[None, :],
            "ln2wc": np.ascontiguousarray(ln2w.reshape(DT, 128).T),
            "ln2bc": np.ascontiguousarray(ln2b.reshape(DT, 128).T),
            "ffnw": ffnw,
            "poff": np.array([[q * Tn]], dtype=np.uint32),
            "poff2": np.array([[q * half]], dtype=np.uint32),
        }
        m.update(consts)
        in_maps.append(m)
    res = run_bass_kernel_spmd(nc, in_maps, core_ids=list(range(8)))
    out = np.empty((x.shape[0], Tn, D), dtype=np.float32)
    for c in range(8):
        b, q = c // 2, c % 2
        out[b, q * half:(q + 1) * half] = np.asarray(res.results[c]["out"], dtype=np.float32)
    return out


def kernel(L0_attn_x, L0_attn_Wq, L0_attn_Wk, L0_attn_Wv, L0_attn_Wb, L0_attn_Wa,
           L0_attn_Wo, L0_attn_ln1_w, L0_attn_ln1_b, L0_attn_ln2_w, L0_attn_ln2_b,
           L0_attn_W1, L0_attn_b1, L0_attn_W2, L0_attn_b2, L0_ffn_weight):
    x = np.ascontiguousarray(np.asarray(L0_attn_x, dtype=np.float32))
    Wq = np.asarray(L0_attn_Wq, dtype=np.float32)
    Wk = np.asarray(L0_attn_Wk, dtype=np.float32)
    Wv = np.asarray(L0_attn_Wv, dtype=np.float32)
    Wa = np.asarray(L0_attn_Wa, dtype=np.float32)
    Wb = np.asarray(L0_attn_Wb, dtype=np.float32)
    Wo = np.ascontiguousarray(np.asarray(L0_attn_Wo, dtype=np.float32))
    ln1w = np.asarray(L0_attn_ln1_w, dtype=np.float32)
    ln1b = np.asarray(L0_attn_ln1_b, dtype=np.float32)
    ln2w = np.asarray(L0_attn_ln2_w, dtype=np.float32)
    ln2b = np.asarray(L0_attn_ln2_b, dtype=np.float32)
    W1 = np.ascontiguousarray(np.asarray(L0_attn_W1, dtype=np.float32))
    b1 = np.asarray(L0_attn_b1, dtype=np.float32)
    W2 = np.ascontiguousarray(np.asarray(L0_attn_W2, dtype=np.float32))
    b2 = np.asarray(L0_attn_b2, dtype=np.float32)
    ffnw = np.ascontiguousarray(np.asarray(L0_ffn_weight, dtype=np.float32))

    return _run(x, Wq, Wk, Wv, Wa, Wb, Wo, ln1w, ln1b, ln2w, ln2b, W1, b1, W2, b2, ffnw)
